# version 13
# baseline (speedup 1.0000x reference)
import contextlib

import numpy as np

import concourse.bass as bass
import concourse.mybir as mybir
from concourse.bass_utils import run_bass_kernel_spmd

F32 = mybir.dt.float32
BF16 = mybir.dt.bfloat16
I32 = mybir.dt.int32
AF = mybir.ActivationFunctionType
ALU = mybir.AluOpType
AXX = mybir.AxisListType.X

NCORES = 8
B, S, D = 2, 2048, 1024
T = B * S
NH, HP = 16, 64
NTT = T // 128
ND = D // 128
FS = 2816
FE = 1408
NE = 8
ESH = 40.0
EPS = 1e-5
CAP = 1536
NSC = CAP // 512
NST = CAP // 128
BIGS = 1.0e5

ENGINES = ("sync", "gpsimd", "tensor", "scalar", "vector")


def dap(t, off, *dims):
    return bass.AP(t, off, [list(d) for d in dims])


class Phase:

    def __init__(self, nc, sems, cnt):
        self.nc = nc
        self.sems = sems
        self.cnt = cnt
        self.ops = {e: [] for e in ENGINES}

    def add(self, eng, fn, waits=(), inc=None, inc_n=1):
        w = [(s, v) for (s, v) in waits if v is not None and v > 0]
        self.ops[eng].append((w, fn, inc, inc_n))
        if inc is not None:
            self.cnt[inc] += inc_n
            return self.cnt[inc]
        return None

    def dma(self, eng, out, in_, chan, waits=(), slow=False):
        v = self.add(
            eng,
            lambda e, o=out, i=in_, s=slow: e.dma_start(
                out=o, in_=i, allow_slow_non_contiguous=s),
            waits=waits,
            inc=chan,
            inc_n=16,
        )
        return (chan, v)

    def st_waits(self):
        return [("CHS0", self.cnt["CHS0"]), ("CHS1", self.cnt["CHS1"])]

    def run(self):
        sems = self.sems
        with self.nc.Block() as blk:
            for eng_name in ENGINES:
                lst = self.ops[eng_name]
                if not lst:
                    continue

                def body(e, lst=lst):
                    last = {}
                    for (waits, fn, inc, inc_n) in lst:
                        for (s, v) in waits:
                            if last.get(s, 0) >= v:
                                continue
                            e.wait_ge(sems[s], v)
                            last[s] = v
                        ins = fn(e)
                        if inc is not None:
                            ins.then_inc(sems[inc], inc_n)

                getattr(blk, eng_name)(body)


def _ln_tile(p, src, st2, ht_out, waits_x, junk, consts):
    a_sq = p.add(
        "scalar",
        lambda e, x=src, s=st2: e.activation(
            out=junk[:, :], in_=x, func=AF.Square, accum_out=s[:, 0:1]),
        waits=waits_x,
        inc="ACT",
    )
    d_sum = p.add(
        "vector",
        lambda e, x=src, s=st2: e.reduce_sum(out=s[:, 1:2], in_=x, axis=AXX),
        waits=waits_x,
        inc="DVE",
    )
    a_mean = p.add(
        "scalar",
        lambda e, s=st2: e.mul(s[:, 2:3], s[:, 1:2], 1.0 / D),
        waits=[("DVE", d_sum)],
        inc="ACT",
    )
    a_m2 = p.add(
        "scalar",
        lambda e, s=st2: e.square(s[:, 3:4], s[:, 2:3]),
        waits=[("ACT", a_mean)],
        inc="ACT",
    )
    d_var = p.add(
        "vector",
        lambda e, s=st2: e.scalar_tensor_tensor(
            out=s[:, 4:5], in0=s[:, 0:1], scalar=1.0 / D, in1=s[:, 3:4],
            op0=ALU.mult, op1=ALU.subtract),
        waits=[("ACT", a_sq), ("ACT", a_m2)],
        inc="DVE",
    )
    a_sd = p.add(
        "scalar",
        lambda e, s=st2: e.activation(
            out=s[:, 5:6], in_=s[:, 4:5], func=AF.Sqrt, bias=consts[:, 0:1]),
        waits=[("DVE", d_var)],
        inc="ACT",
    )
    d_rec = p.add(
        "vector",
        lambda e, s=st2: e.reciprocal(out=s[:, 6:7], in_=s[:, 5:6]),
        waits=[("ACT", a_sd)],
        inc="DVE",
    )
    return d_rec


def _scatter_steps(p, experts, h2t, xwt, h2tmd, xar, xwar, posT_i, wTt_f,
                   d_pti, d_gi):
    hold = {}

    def _setup(e):
        bc = e.alloc_register()
        e.reg_mov(bc, CAP - 1)
        hold["bc"] = bc
        return e.nop()

    p.add("gpsimd", _setup)
    pair_lm = {}
    experts = list(experts)
    steps = []

    def mk_step(duo, tile):
        def step():
            for ee in duo:
                gs = ee % 2
                w = [("DVE", d_pti), ("DVE", d_gi)]
                if (ee, tile - 1) in pair_lm:
                    w.append(pair_lm[(ee, tile - 1)])
                elif (ee - 2, NTT - 1) in pair_lm:
                    w.append(pair_lm[(ee - 2, NTT - 1)])
                ldh = p.dma("sync", h2t[:, gs, :],
                            h2tmd.ap()[128 * tile:128 * tile + 128, :],
                            f"CHL{4 + gs}", waits=w)
                d_xw = p.add(
                    "vector",
                    lambda e, tile=tile, ee=ee, gs=gs: e.tensor_scalar(
                        out=xwt[:, gs, :], in0=h2t[:, gs, :],
                        scalar1=wTt_f[:, tile, ee:ee + 1], scalar2=None,
                        op0=ALU.mult),
                    waits=w + [ldh],
                    inc="DVE",
                )
                p.add(
                    "gpsimd",
                    lambda e, tile=tile, ee=ee, gs=gs: e.indirect_dma_start(
                        out=xar.ap(),
                        out_offset=bass.IndirectOffsetOnAxis(
                            posT_i[:, tile, ee:ee + 1], 0),
                        in_=h2t[:, gs, :],
                        in_offset=None,
                        element_offset=ee * CAP * D,
                        bounds_check=hold["bc"],
                        oob_is_err=False),
                    waits=[ldh],
                    inc=f"CHX{ee}",
                    inc_n=16,
                )
                sxw = p.add(
                    "gpsimd",
                    lambda e, tile=tile, ee=ee, gs=gs: e.indirect_dma_start(
                        out=xwar.ap(),
                        out_offset=bass.IndirectOffsetOnAxis(
                            posT_i[:, tile, ee:ee + 1], 0),
                        in_=xwt[:, gs, :],
                        in_offset=None,
                        element_offset=ee * CAP * D,
                        bounds_check=hold["bc"],
                        oob_is_err=False),
                    waits=[("DVE", d_xw)],
                    inc=f"CHX{ee}",
                    inc_n=16,
                )
                pair_lm[(ee, tile)] = (f"CHX{ee}", sxw)
        return step

    for e0 in range(0, len(experts), 2):
        duo = experts[e0:e0 + 2]
        for tile in range(NTT):
            steps.append(mk_step(duo, tile))
    return steps


def build():
    nc = bass.Bass("TRN2", target_bir_lowering=False)

    x_in = nc.dram_tensor("x", [T, D], F32, kind="ExternalInput").ap()
    wqkv = nc.dram_tensor("wqkv", [D, 3 * D], F32, kind="ExternalInput")
    bqkv = nc.dram_tensor("bqkv", [3 * D], F32, kind="ExternalInput")
    gatew = nc.dram_tensor("gatew", [NE, D], F32, kind="ExternalInput").ap()
    egu = nc.dram_tensor("egu", [NE, 2 * FE, D], F32, kind="ExternalInput").ap()
    edown = nc.dram_tensor("edown", [NE, D, FE], F32, kind="ExternalInput").ap()
    wg = nc.dram_tensor("wg", [FS, D], F32, kind="ExternalInput").ap()
    wu = nc.dram_tensor("wu", [FS, D], F32, kind="ExternalInput").ap()
    wd = nc.dram_tensor("wd", [D, FS], F32, kind="ExternalInput").ap()
    segw = nc.dram_tensor("segw", [1, D], F32, kind="ExternalInput").ap()
    out_t = nc.dram_tensor("out", [T, D], F32, kind="ExternalOutput").ap()

    qdh = nc.dram_tensor("qdh", [D, T], BF16)
    qdl = nc.dram_tensor("qdl", [D, T], BF16)
    kdh = nc.dram_tensor("kdh", [D, T], BF16)
    kdl = nc.dram_tensor("kdl", [D, T], BF16)
    vd = nc.dram_tensor("vd", [T, D], F32)
    attnd = nc.dram_tensor("attnd", [T, D], F32)
    x1d = nc.dram_tensor("x1d", [T, D], F32)
    wTd = nc.dram_tensor("wTd", [9, T], F32)
    hidd_sh = nc.dram_tensor("hidd_sh", [FS, T], BF16)
    hidd_sl = nc.dram_tensor("hidd_sl", [FS, T], BF16)
    h2tmd = nc.dram_tensor("h2tmd", [T, D], F32)
    xar = nc.dram_tensor("xar", [NE * CAP, D], F32)
    xwar = nc.dram_tensor("xwar", [NE * CAP, D], F32)
    dar = nc.dram_tensor("dar", [NE * CAP, D], F32)
    shoutd = nc.dram_tensor("shoutd", [T, D], F32)

    cnt = {"CHC": 0, "CHL0": 0, "CHL1": 0, "CHL2": 0, "CHL3": 0,
           "CHL4": 0, "CHL5": 0,
           "CHS0": 0, "CHS1": 0, "PE": 0, "ACT": 0, "DVE": 0, "GPS": 0}
    for _e in range(NE):
        cnt[f"CHX{_e}"] = 0

    with contextlib.ExitStack() as top:
        sems = {n: top.enter_context(nc.semaphore(n)) for n in cnt}
        ident = top.enter_context(nc.sbuf_tensor("ident", [128, 128], F32))
        identb = top.enter_context(nc.sbuf_tensor("identb", [128, 128], BF16))
        consts = top.enter_context(nc.sbuf_tensor("consts", [128, 2], F32))
        eiota = top.enter_context(nc.sbuf_tensor("eiota", [128, 8], F32))
        zsb = top.enter_context(nc.sbuf_tensor("zsb", [128, 1024], F32))
        posT_i = top.enter_context(nc.sbuf_tensor("posT_i", [128, NTT, 8], I32))
        wTt_f = top.enter_context(nc.sbuf_tensor("wTt_f", [128, NTT, 8], F32))
        gidx_i = top.enter_context(nc.sbuf_tensor("gidx_i", [128, NTT, 2], I32))

        p = Phase(nc, sems, cnt)
        with nc.sbuf_tensor("iob", [128, 128], F32) as iob:
            p.add("gpsimd", lambda e: e.memset(consts[:, 0:1], EPS), inc="GPS")
            p.add("gpsimd", lambda e: e.memset(consts[:, 1:2], -ESH), inc="GPS")
            p.add("gpsimd", lambda e: e.iota(
                eiota[:, :], [[1, 8]], channel_multiplier=0,
                allow_small_or_imprecise_dtypes=True), inc="GPS")
            g_zs = p.add("gpsimd", lambda e: e.memset(zsb[:, :], 0.0), inc="GPS")
            g_iota = p.add(
                "gpsimd",
                lambda e: e.iota(iob[:, :], [[1, 128]], channel_multiplier=-1,
                                 allow_small_or_imprecise_dtypes=True),
                inc="GPS",
            )
            for ar in (xar, xwar):
                for r0 in range(0, NE * CAP, 1024):
                    p.dma(
                        "sync",
                        dap(ar, r0 * D, (D, 128), (128 * D, 8), (1, D)),
                        bass.AP(zsb, 0, [[1024, 128], [0, 8], [1, 1024]]),
                        "CHL2" if ar is xar else "CHL3",
                        waits=[("GPS", g_zs)])
            zf2, zf3 = ("CHL2", cnt["CHL2"]), ("CHL3", cnt["CHL3"])
            dvi = p.add(
                "vector",
                lambda e: e.tensor_scalar(out=ident[:, :], in0=iob[:, :],
                                          scalar1=0.0, scalar2=None,
                                          op0=ALU.is_equal),
                waits=[("GPS", g_iota)],
                inc="DVE",
            )
            p.add(
                "vector",
                lambda e: e.tensor_copy(out=identb[:, :], in_=ident[:, :]),
                waits=[("DVE", dvi)],
                inc="DVE",
            )
            p.run()

        with contextlib.ExitStack() as st1:
            h1 = st1.enter_context(nc.sbuf_tensor("h1", [128, ND, T], F32))
            b_sb = st1.enter_context(nc.sbuf_tensor("b_sb", [128, 16], F32))
            bvb = st1.enter_context(nc.sbuf_tensor("bvb", [128, D], F32))

            p = Phase(nc, sems, cnt)
            with contextlib.ExitStack() as ps:
                xt = ps.enter_context(nc.sbuf_tensor("xt", [128, 2, D], F32))
                junk = ps.enter_context(nc.sbuf_tensor("junk", [128, D], F32))
                stats = ps.enter_context(nc.sbuf_tensor("stats", [128, 2, 8], F32))
                ht = ps.enter_context(nc.sbuf_tensor("ht", [128, 2, D], F32))
                pbank = [ps.enter_context(nc.psum_tensor(f"p1{i}", [128, 512], F32))
                         for i in range(2)]
                bsb_ld = p.dma("sync", b_sb[:, :],
                               dap(bqkv, 0, (1, 128), (128, 16)), "CHC",
                               slow=True)
                bvb_ld = p.dma("sync", bvb[:, :],
                               dap(bqkv, 2 * D, (0, 128), (1, D)), "CHC")
                ld = {}
                ht_done = {}
                tr_done = {}
                ev_done = {}
                for tt in range(NTT):
                    w = [("DVE", ht_done[tt - 2])] if tt >= 2 else []
                    ld[tt] = p.dma("sync", xt[:, tt % 2, :],
                                   x_in[128 * tt:128 * tt + 128, :],
                                   f"CHL{tt % 2}", waits=w)
                    st2 = stats[:, tt % 2, :]
                    src = xt[:, tt % 2, :]
                    d_rec = _ln_tile(p, src, st2, None, [ld[tt]], junk,
                                     consts)
                    w = [("PE", tr_done[tt - 2])] if tt >= 2 else []
                    w.append(("DVE", d_rec))
                    ht_done[tt] = p.add(
                        "vector",
                        lambda e, x=src, s=st2, h=ht[:, tt % 2, :]: e.tensor_scalar(
                            out=h, in0=x, scalar1=s[:, 2:3], scalar2=s[:, 6:7],
                            op0=ALU.subtract, op1=ALU.mult),
                        waits=w,
                        inc="DVE",
                    )
                    for half in range(2):
                        pb = pbank[half]
                        pe_t = None
                        for dd in range(4):
                            d = 4 * half + dd
                            w = [("DVE", ht_done[tt])]
                            if tt >= 1:
                                w.append(("DVE", ev_done[(tt - 1, half)]))
                            pe_t = p.add(
                                "tensor",
                                lambda e, d=d, dd=dd, pb=pb, h=ht[:, tt % 2, :]:
                                    e.transpose(pb[:, 128 * dd:128 * dd + 128],
                                                h[:, 128 * d:128 * d + 128],
                                                ident[:, :]),
                                waits=w,
                                inc="PE",
                            )
                        tr_done[tt] = pe_t
                        ev_done[(tt, half)] = p.add(
                            "vector",
                            lambda e, pb=pb, tt=tt, half=half: e.tensor_copy(
                                out=h1[:, 4 * half:4 * half + 4,
                                       128 * tt:128 * tt + 128],
                                in_=pb[:, :].rearrange("p (a b) -> p a b", a=4)),
                            waits=[("PE", pe_t)],
                            inc="DVE",
                        )
                p.run()

            p = Phase(nc, sems, cnt)
            with contextlib.ExitStack() as ps:
                wbuf = ps.enter_context(nc.sbuf_tensor("wbuf", [128, 2, ND, 128], F32))
                qkstage = ps.enter_context(nc.sbuf_tensor("qkstage", [128, 2, 512], F32))
                qksh = ps.enter_context(nc.sbuf_tensor("qksh", [128, 2, 512], BF16))
                qksl = ps.enter_context(nc.sbuf_tensor("qksl", [128, 2, 512], BF16))
                pbank = [ps.enter_context(nc.psum_tensor(f"p2{i}", [128, 512], F32))
                         for i in range(2)]
                ldm = {}
                pe_end = {}
                act_ev = {}
                psum_ev = {}
                st_done = {}
                for m in range(16):
                    w = [("PE", pe_end[(m - 2, 7)])] if m >= 2 else []
                    ldm[m] = p.dma(
                        "sync", wbuf[:, m % 2, :, :],
                        dap(wqkv, 128 * m, (3 * D, 128), (128 * 3 * D, ND), (1, 128)),
                        f"CHL{m % 2}", waits=w)
                    for n in range(8):
                        idx = 8 * m + n
                        pb = pbank[idx % 2]
                        for d in range(ND):
                            w = [ldm[m]]
                            if idx >= 2 and d == 0:
                                w.append(("ACT", psum_ev[idx - 2]))
                            pe_end[(m, n)] = p.add(
                                "tensor",
                                lambda e, pb=pb, m=m, n=n, d=d: e.matmul(
                                    pb[:, :], wbuf[:, m % 2, d, :],
                                    h1[:, d, 512 * n:512 * n + 512],
                                    start=(d == 0), stop=(d == ND - 1)),
                                waits=w,
                                inc="PE" if d == ND - 1 else None,
                            )
                        w = [("PE", pe_end[(m, n)]), bvb_ld]
                        if idx >= 2:
                            w.append(("DVE", act_ev[idx - 2]))
                        psum_ev[idx] = p.add(
                            "scalar",
                            lambda e, pb=pb, m=m, idx=idx: e.activation(
                                out=qkstage[:, idx % 2, :], in_=pb[:, :],
                                func=AF.Identity, bias=b_sb[:, m:m + 1]),
                            waits=w,
                            inc="ACT",
                        )
                        w = [("ACT", psum_ev[idx])]
                        if idx >= 2:
                            w.append(st_done[idx - 2])
                        a_hi = p.add(
                            "scalar",
                            lambda e, idx=idx: e.copy(
                                out=qksh[:, idx % 2, :],
                                in_=qkstage[:, idx % 2, :]),
                            waits=w,
                            inc="ACT",
                        )
                        act_ev[idx] = p.add(
                            "vector",
                            lambda e, idx=idx: e.tensor_tensor(
                                out=qksl[:, idx % 2, :],
                                in0=qkstage[:, idx % 2, :],
                                in1=qksh[:, idx % 2, :], op=ALU.subtract),
                            waits=[("ACT", a_hi)],
                            inc="DVE",
                        )
                        dsth = qdh if m < 8 else kdh
                        dstl = qdl if m < 8 else kdl
                        p.dma(
                            "gpsimd",
                            dap(dsth, (128 * (m % 8)) * T + 512 * n, (T, 128),
                                (1, 512)),
                            qksh[:, idx % 2, :], f"CHS{idx % 2}",
                            waits=[("DVE", act_ev[idx])])
                        st_done[idx] = p.dma(
                            "gpsimd",
                            dap(dstl, (128 * (m % 8)) * T + 512 * n, (T, 128),
                                (1, 512)),
                            qksl[:, idx % 2, :], f"CHS{idx % 2}",
                            waits=[("DVE", act_ev[idx])])
                p.add("gpsimd", lambda e: e.nop(), waits=p.st_waits())
                p.run()

            p = Phase(nc, sems, cnt)
            with contextlib.ExitStack() as ps:
                wv = ps.enter_context(nc.sbuf_tensor("wv", [128, ND, D], F32))
                vstage = ps.enter_context(nc.sbuf_tensor("vstage", [128, 2, D], F32))
                pv = [ps.enter_context(nc.psum_tensor(f"p2v{i}", [128, 512], F32))
                      for i in range(4)]
                ldv = p.dma(
                    "sync", wv[:, :, :],
                    dap(wqkv, 2 * D, (3 * D, 128), (128 * 3 * D, ND), (1, D)),
                    "CHC")
                pe_end = {}
                dve_ev = {}
                st_done = {}
                for tt in range(NTT):
                    for c in range(2):
                        pb = pv[2 * (tt % 2) + c]
                        for d in range(ND):
                            w = [ldv]
                            if tt >= 2 and d == 0:
                                w.append(("DVE", dve_ev[(tt - 2, c)]))
                            pe_end[(tt, c)] = p.add(
                                "tensor",
                                lambda e, pb=pb, tt=tt, c=c, d=d: e.matmul(
                                    pb[:, :], h1[:, d, 128 * tt:128 * tt + 128],
                                    wv[:, d, 512 * c:512 * c + 512],
                                    start=(d == 0), stop=(d == ND - 1)),
                                waits=w,
                                inc="PE" if d == ND - 1 else None,
                            )
                        w = [("PE", pe_end[(tt, c)]), bvb_ld]
                        if tt >= 2:
                            w.append(st_done[tt - 2])
                        dve_ev[(tt, c)] = p.add(
                            "vector",
                            lambda e, pb=pb, tt=tt, c=c: e.tensor_tensor(
                                out=vstage[:, tt % 2, 512 * c:512 * c + 512],
                                in0=pb[:, :], in1=bvb[:, 512 * c:512 * c + 512],
                                op=ALU.add),
                            waits=w,
                            inc="DVE",
                        )
                    st_done[tt] = p.dma(
                        "gpsimd", vd.ap()[128 * tt:128 * tt + 128, :],
                        vstage[:, tt % 2, :], f"CHS{tt % 2}",
                        waits=[("DVE", dve_ev[(tt, 1)])])
                p.add("gpsimd", lambda e: e.nop(), waits=p.st_waits())
                p.run()

        with contextlib.ExitStack() as st2s:
            q_hh = st2s.enter_context(nc.sbuf_tensor("q_hh", [128, S], BF16))
            q_ll = st2s.enter_context(nc.sbuf_tensor("q_ll", [128, S], BF16))
            k_st = st2s.enter_context(nc.sbuf_tensor("k_st", [128, S], BF16))
            v65 = st2s.enter_context(nc.sbuf_tensor("v65", [128, 16, 65], F32))
            p_sb = st2s.enter_context(nc.sbuf_tensor("p_sb", [128, 16, 512], F32))
            o_raw = st2s.enter_context(nc.sbuf_tensor("o_raw", [128, 2, 512], F32))
            a_stage = st2s.enter_context(
                nc.sbuf_tensor("a_stage", [128, 2, 4, 64], F32))
            rs = st2s.enter_context(nc.sbuf_tensor("rs", [128, 2, 4], F32))
            pS = [st2s.enter_context(nc.psum_tensor(f"pS{i}", [128, 512], F32))
                  for i in range(3)]
            pO = [st2s.enter_context(nc.psum_tensor(f"pO{i}", [128, 512], F32))
                  for i in range(2)]
            pT = [st2s.enter_context(nc.psum_tensor(f"pT{i}", [128, 512], F32))
                  for i in range(2)]
            for s in range(B):
                for h in range(NH):
                    p = Phase(nc, sems, cnt)
                    p.dma("sync", q_hh[0:64, :],
                          dap(qdh, (64 * h) * T + S * s, (T, 64), (1, S)),
                          "CHL0")
                    p.dma("sync", q_hh[64:128, :],
                          dap(qdh, (64 * h) * T + S * s, (T, 64), (1, S)),
                          "CHL0")
                    p.dma("sync", q_ll[0:64, :],
                          dap(qdl, (64 * h) * T + S * s, (T, 64), (1, S)),
                          "CHL0")
                    ldq = p.dma("sync", q_ll[64:128, :],
                                dap(qdl, (64 * h) * T + S * s, (T, 64), (1, S)),
                                "CHL0")
                    p.dma("sync", k_st[0:64, :],
                          dap(kdh, (64 * h) * T + S * s, (T, 64), (1, S)),
                          "CHL1")
                    ldk = p.dma("sync", k_st[64:128, :],
                                dap(kdl, (64 * h) * T + S * s, (T, 64), (1, S)),
                                "CHL1")
                    ldvl = None
                    for j in range(16):
                        ldvl = p.dma(
                            "sync", v65[:, j, 0:64],
                            dap(vd, (S * s + 128 * j) * D + 64 * h, (D, 128),
                                (1, 64)),
                            "CHL2")
                    g_ones = p.add(
                        "gpsimd",
                        lambda e: e.memset(v65[:, :, 64:65], 1.0),
                        inc="GPS",
                    )
                    exp_a = {}
                    smm = {}
                    pv_end = {}
                    oev = {}
                    tr_end = {}
                    scl = {}
                    stn = {}
                    for ic in range(4):
                        for j in range(16):
                            idx = 16 * ic + j
                            w = [ldk, ldq]
                            if idx >= 3:
                                w.append(("ACT", exp_a[idx - 3]))
                            p.add(
                                "tensor",
                                lambda e, ic=ic, j=j, pb=pS[idx % 3]: e.matmul(
                                    pb[:, :], k_st[:, 128 * j:128 * j + 128],
                                    q_hh[:, 512 * ic:512 * ic + 512],
                                    start=True, stop=False),
                                waits=w,
                            )
                            smm[idx] = p.add(
                                "tensor",
                                lambda e, ic=ic, j=j, pb=pS[idx % 3]: e.matmul(
                                    pb[:, :], k_st[:, 128 * j:128 * j + 128],
                                    q_ll[:, 512 * ic:512 * ic + 512],
                                    start=False, stop=True),
                                inc="PE",
                            )
                            w = [("PE", smm[idx])]
                            if ic >= 1 and j == 0:
                                w.append(("PE", pv_end[ic - 1]))
                            exp_a[idx] = p.add(
                                "scalar",
                                lambda e, j=j, pb=pS[idx % 3]: e.activation(
                                    out=p_sb[:, j, :], in_=pb[:, :], func=AF.Exp,
                                    bias=consts[:, 1:2]),
                                waits=w,
                                inc="ACT",
                            )
                        for j in range(16):
                            idx = 16 * ic + j
                            w = [("ACT", exp_a[idx]), ldvl,
                                 ("GPS", g_ones)]
                            if ic >= 2 and j == 0:
                                w.append(("DVE", oev[ic - 2]))
                            pv_end[ic] = p.add(
                                "tensor",
                                lambda e, ic=ic, j=j, pb=pO[ic % 2]: e.matmul(
                                    pb[0:65, :], v65[:, j, :], p_sb[:, j, :],
                                    start=(j == 0), stop=(j == 15)),
                                waits=w,
                                inc="PE" if j == 15 else None,
                            )
                        oev[ic] = p.add(
                            "vector",
                            lambda e, ic=ic: e.tensor_copy(
                                out=o_raw[0:65, ic % 2, :],
                                in_=pO[ic % 2][0:65, :]),
                            waits=[("PE", pv_end[ic])],
                            inc="DVE",
                        )
                        for bb in range(4):
                            w = [("DVE", oev[ic])]
                            if ic >= 2 and bb == 0:
                                w.append(("DVE", scl[ic - 2]))
                            tr_end[ic] = p.add(
                                "tensor",
                                lambda e, ic=ic, bb=bb: e.transpose(
                                    pT[ic % 2][:, 65 * bb:65 * bb + 65],
                                    o_raw[0:65, ic % 2, 128 * bb:128 * bb + 128],
                                    ident[0:65, 0:65]),
                                waits=w,
                                inc="PE" if bb == 3 else None,
                            )
                        d_rec = p.add(
                            "vector",
                            lambda e, ic=ic, pb=pT[ic % 2]: e.reciprocal(
                                out=rs[:, ic % 2, :],
                                in_=bass.AP(pb, 64, [[512, 128], [65, 4]])),
                            waits=[("PE", tr_end[ic])],
                            inc="DVE",
                        )
                        for bb in range(4):
                            w = [("DVE", d_rec)]
                            if ic >= 2 and bb == 0:
                                w.append(stn[ic - 2])
                            scl[ic] = p.add(
                                "vector",
                                lambda e, ic=ic, bb=bb, pb=pT[ic % 2]:
                                    e.tensor_scalar(
                                        out=a_stage[:, ic % 2, bb, :],
                                        in0=pb[:, 65 * bb:65 * bb + 64],
                                        scalar1=rs[:, ic % 2, bb:bb + 1],
                                        scalar2=None, op0=ALU.mult),
                                waits=w,
                                inc="DVE",
                            )
                        stn[ic] = p.dma(
                            "gpsimd",
                            dap(attnd, (S * s + 128 * h) * D + 512 * ic * 64,
                                (64, 128), (128 * 64, 4), (1, 64)),
                            a_stage[:, ic % 2, :, :], f"CHS{ic % 2}",
                            waits=[("DVE", scl[ic])])
                    p.add("gpsimd", lambda e: e.nop(), waits=p.st_waits())
                    p.run()

        with contextlib.ExitStack() as st3:
            h2h = st3.enter_context(nc.sbuf_tensor("h2h", [128, ND, T], BF16))
            h2l = st3.enter_context(nc.sbuf_tensor("h2l", [128, ND, T], BF16))

            p = Phase(nc, sems, cnt)
            with contextlib.ExitStack() as ps:
                xt = ps.enter_context(nc.sbuf_tensor("xt4", [128, 2, D], F32))
                at = ps.enter_context(nc.sbuf_tensor("at4", [128, 2, D], F32))
                x1t = ps.enter_context(nc.sbuf_tensor("x1t", [128, 2, D], F32))
                junk = ps.enter_context(nc.sbuf_tensor("junk4", [128, D], F32))
                stats = ps.enter_context(nc.sbuf_tensor("stats4", [128, 2, 8], F32))
                ht = ps.enter_context(nc.sbuf_tensor("ht4", [128, 2, D], F32))
                hth = ps.enter_context(nc.sbuf_tensor("hth4", [128, 2, D], BF16))
                htl = ps.enter_context(nc.sbuf_tensor("htl4", [128, 2, D], BF16))
                pbank = [ps.enter_context(
                    nc.psum_tensor(f"p4{i}", [128, 512], BF16))
                    for i in range(2)]
                x1_done = {}
                ht_done = {}
                tr_done = {}
                ev_done = {}
                st_done = {}
                st2_h2 = {}
                for tt in range(NTT):
                    w = [("DVE", x1_done[tt - 2])] if tt >= 2 else []
                    ldx = p.dma("sync", xt[:, tt % 2, :],
                                x_in[128 * tt:128 * tt + 128, :],
                                f"CHL{tt % 2}", waits=w)
                    lda = p.dma("sync", at[:, tt % 2, :],
                                attnd.ap()[128 * tt:128 * tt + 128, :],
                                f"CHL{2 + tt % 2}", waits=w)
                    w = [ldx, lda]
                    if tt >= 2:
                        w.append(st2_h2[tt - 2])
                    x1_done[tt] = p.add(
                        "vector",
                        lambda e, tt=tt: e.tensor_tensor(
                            out=x1t[:, tt % 2, :], in0=xt[:, tt % 2, :],
                            in1=at[:, tt % 2, :], op=ALU.add),
                        waits=w,
                        inc="DVE",
                    )
                    st_done[tt] = p.dma(
                        "gpsimd", x1d.ap()[128 * tt:128 * tt + 128, :],
                        x1t[:, tt % 2, :], f"CHS{tt % 2}",
                        waits=[("DVE", x1_done[tt])])
                    st2 = stats[:, tt % 2, :]
                    src = x1t[:, tt % 2, :]
                    d_rec = _ln_tile(p, src, st2, None, [("DVE", x1_done[tt])],
                                     junk, consts)
                    w = [("PE", tr_done[tt - 2])] if tt >= 2 else []
                    w.append(("DVE", d_rec))
                    if tt >= 2:
                        w.append(st2_h2[tt - 2])
                    ht_done[tt] = p.add(
                        "vector",
                        lambda e, x=src, s=st2, hh=ht[:, tt % 2, :]: e.tensor_scalar(
                            out=hh, in0=x, scalar1=s[:, 2:3], scalar2=s[:, 6:7],
                            op0=ALU.subtract, op1=ALU.mult),
                        waits=w,
                        inc="DVE",
                    )
                    st2_h2[tt] = p.dma(
                        "gpsimd", h2tmd.ap()[128 * tt:128 * tt + 128, :],
                        ht[:, tt % 2, :], f"CHS{tt % 2}",
                        waits=[("DVE", ht_done[tt])])
                    s_hi = p.add(
                        "scalar",
                        lambda e, tt=tt: e.copy(out=hth[:, tt % 2, :],
                                                in_=ht[:, tt % 2, :]),
                        waits=[("DVE", ht_done[tt])] + (
                            [("PE", tr_done[tt - 2])] if tt >= 2 else []),
                        inc="ACT",
                    )
                    d_lo = p.add(
                        "vector",
                        lambda e, tt=tt: e.tensor_tensor(
                            out=htl[:, tt % 2, :], in0=ht[:, tt % 2, :],
                            in1=hth[:, tt % 2, :], op=ALU.subtract),
                        waits=[("ACT", s_hi)],
                        inc="DVE",
                    )
                    for half in range(2):
                        pb = pbank[half]
                        pe_t = None
                        prev_ev = None
                        for (srcb, dstb) in ((hth, h2h), (htl, h2l)):
                            for dd in range(4):
                                d = 4 * half + dd
                                w = [("DVE", d_lo)]
                                if prev_ev is not None:
                                    w.append(("DVE", prev_ev))
                                elif tt >= 1:
                                    w.append(("DVE", ev_done[(tt - 1, half)]))
                                pe_t = p.add(
                                    "tensor",
                                    lambda e, d=d, dd=dd, pb=pb,
                                           hh=srcb[:, tt % 2, :]:
                                        e.transpose(
                                            pb[:, 128 * dd:128 * dd + 128],
                                            hh[:, 128 * d:128 * d + 128],
                                            identb[:, :]),
                                    waits=w,
                                    inc="PE",
                                )
                            ev_done[(tt, half)] = p.add(
                                "vector",
                                lambda e, pb=pb, tt=tt, half=half, dstb=dstb:
                                    e.tensor_copy(
                                        out=dstb[:, 4 * half:4 * half + 4,
                                                 128 * tt:128 * tt + 128],
                                        in_=pb[:, :].rearrange(
                                            "p (a b) -> p a b", a=4)),
                                waits=[("PE", pe_t)],
                                inc="DVE",
                            )
                            prev_ev = ev_done[(tt, half)]
                        tr_done[tt] = pe_t
                p.add("gpsimd", lambda e: e.nop(), waits=p.st_waits())
                p.run()

            p = Phase(nc, sems, cnt)
            with contextlib.ExitStack() as ps:
                gw_s = ps.enter_context(nc.sbuf_tensor("gw_s", [9, D], F32))
                gwTh = ps.enter_context(nc.sbuf_tensor("gwTh", [128, ND * 16], BF16))
                gwTl = ps.enter_context(nc.sbuf_tensor("gwTl", [128, ND * 16], BF16))
                wT_sb = ps.enter_context(nc.sbuf_tensor("wT_sb", [9, T], F32))
                lg = ps.enter_context(nc.sbuf_tensor("lg", [16, 2, 512], F32))
                lt = ps.enter_context(nc.sbuf_tensor("lt", [128, 2, 4, 16], F32))
                rbuf = ps.enter_context(nc.sbuf_tensor("rbuf", [128, 40], F32))
                eT = ps.enter_context(nc.sbuf_tensor("eT", [128, NTT, 2], F32))
                indT = ps.enter_context(nc.sbuf_tensor("indT", [8, T], F32))
                posb = ps.enter_context(nc.sbuf_tensor("posb", [8, T], F32))
                posT_f = ps.enter_context(
                    nc.sbuf_tensor("posT_f", [128, NTT, 8], F32))
                gidx_f = ps.enter_context(
                    nc.sbuf_tensor("gidx_f", [128, NTT, 2], F32))
                w16 = ps.enter_context(nc.sbuf_tensor("w16", [128, 2, 4, 16], F32))
                pL = [ps.enter_context(nc.psum_tensor(f"p5l{i}", [128, 512], F32))
                      for i in range(2)]
                pT2 = [ps.enter_context(nc.psum_tensor(f"p5t{i}", [128, 512], F32))
                       for i in range(2)]
                pW = [ps.enter_context(nc.psum_tensor(f"p5w{i}", [128, 512], F32))
                      for i in range(2)]
                ldg = p.dma("sync", gw_s[0:8, :], gatew, "CHC")
                ldg = p.dma("sync", gw_s[8:9, :], segw, "CHC")
                gwt_ev = {}
                for d in range(ND):
                    w = [ldg]
                    if d >= 2:
                        w.append(("DVE", gwt_ev[d - 2]))
                    pe_t = p.add(
                        "tensor",
                        lambda e, d=d, pb=pW[d % 2]: e.transpose(
                            pb[:, 0:9], gw_s[0:9, 128 * d:128 * d + 128],
                            ident[0:9, 0:9]),
                        waits=w,
                        inc="PE",
                    )
                    g_hi = p.add(
                        "scalar",
                        lambda e, d=d, pb=pW[d % 2]: e.copy(
                            out=gwTh[:, 16 * d:16 * d + 9], in_=pb[:, 0:9]),
                        waits=[("PE", pe_t)],
                        inc="ACT",
                    )
                    gwt_ev[d] = p.add(
                        "vector",
                        lambda e, d=d, pb=pW[d % 2]: e.scalar_tensor_tensor(
                            out=gwTl[:, 16 * d:16 * d + 9], in0=pb[:, 0:9],
                            scalar=1.0, in1=gwTh[:, 16 * d:16 * d + 9],
                            op0=ALU.mult, op1=ALU.subtract),
                        waits=[("ACT", g_hi)],
                        inc="DVE",
                    )
                gwt_last = gwt_ev[ND - 1]
                lg_ev = {}
                lt_ev = {}
                w_done = {}
                wt_ev = {}
                for tc in range(8):
                    mm_end = None
                    np3 = 0
                    for (ga, hb) in ((0, 0), (0, 1), (1, 0)):
                        gsrc = (gwTh if ga == 0 else gwTl)
                        hsrc = (h2h if hb == 0 else h2l)
                        for d in range(ND):
                            w = [("DVE", gwt_last)]
                            if tc >= 2 and np3 == 0:
                                w.append(("DVE", lg_ev[tc - 2]))
                            mm_end = p.add(
                                "tensor",
                                lambda e, tc=tc, d=d, gs=gsrc, hs=hsrc,
                                       st=(np3 == 0), sp=(np3 == 23),
                                       pb=pL[tc % 2]: e.matmul(
                                    pb[0:9, :], gs[:, 16 * d:16 * d + 9],
                                    hs[:, d, 512 * tc:512 * tc + 512],
                                    start=st, stop=sp),
                                waits=w,
                                inc="PE" if np3 == 23 else None,
                            )
                            np3 += 1
                    lg_ev[tc] = p.add(
                        "vector",
                        lambda e, tc=tc, pb=pL[tc % 2]: e.tensor_copy(
                            out=lg[0:9, tc % 2, :], in_=pb[0:9, :]),
                        waits=[("PE", mm_end)],
                        inc="DVE",
                    )
                    t_end = None
                    for bb in range(4):
                        w = [("DVE", lg_ev[tc])]
                        if tc >= 2 and bb == 0:
                            w.append(("DVE", lt_ev[tc - 2]))
                        t_end = p.add(
                            "tensor",
                            lambda e, tc=tc, bb=bb, pb=pT2[tc % 2]: e.transpose(
                                pb[:, 9 * bb:9 * bb + 9],
                                lg[0:9, tc % 2, 128 * bb:128 * bb + 128],
                                ident[0:9, 0:9]),
                            waits=w,
                            inc="PE" if bb == 3 else None,
                        )
                    w = [("PE", t_end)]
                    if tc >= 2:
                        w.append(("ACT", w_done[tc - 2]))
                    lt_ev[tc] = p.add(
                        "vector",
                        lambda e, tc=tc, pb=pT2[tc % 2]: e.tensor_copy(
                            out=lt[:, tc % 2, :, 0:9],
                            in_=pb[:, 0:36].rearrange("p (b n) -> p b n", b=4)),
                        waits=w,
                        inc="DVE",
                    )
                    d_w = None
                    for bb in range(4):
                        ltb = lt[:, tc % 2, bb, :]
                        d1 = p.add(
                            "vector",
                            lambda e, ltb=ltb: e.reduce_max(
                                out=rbuf[:, 8:9], in_=ltb[:, 0:8], axis=AXX,
                                negate=True),
                            waits=[("DVE", lt_ev[tc])],
                            inc="DVE",
                        )
                        a1 = p.add(
                            "scalar",
                            lambda e, ltb=ltb: e.activation(
                                out=rbuf[:, 0:8], in_=ltb[:, 0:8], func=AF.Exp,
                                bias=rbuf[:, 8:9], accum_out=rbuf[:, 9:10]),
                            waits=[("DVE", d1)],
                            inc="ACT",
                        )
                        d2 = p.add(
                            "vector",
                            lambda e: e.reciprocal(out=rbuf[:, 10:11],
                                                   in_=rbuf[:, 9:10]),
                            waits=[("ACT", a1)],
                            inc="DVE",
                        )
                        dsc = p.add(
                            "vector",
                            lambda e: e.tensor_scalar(
                                out=rbuf[:, 0:8], in0=rbuf[:, 0:8],
                                scalar1=rbuf[:, 10:11], scalar2=None,
                                op0=ALU.mult),
                            waits=[("DVE", d2)],
                            inc="DVE",
                        )
                        prevd = dsc
                        for ee in range(8):
                            prevd = p.add(
                                "vector",
                                lambda e, ee=ee: e.tensor_scalar(
                                    out=rbuf[:, 16:24], in0=rbuf[:, 0:8],
                                    scalar1=rbuf[:, ee:ee + 1], scalar2=None,
                                    op0=ALU.is_gt, op1=ALU.add,
                                    accum_out=rbuf[:, 12:13]),
                                waits=[("DVE", prevd)],
                                inc="DVE",
                            )
                            if ee > 0:
                                prevd = p.add(
                                    "vector",
                                    lambda e, ee=ee: e.tensor_scalar(
                                        out=rbuf[:, 16:16 + ee], in0=rbuf[:, 0:ee],
                                        scalar1=rbuf[:, ee:ee + 1], scalar2=None,
                                        op0=ALU.is_equal, op1=ALU.add,
                                        accum_out=rbuf[:, 13:14]),
                                    waits=[("DVE", prevd)],
                                    inc="DVE",
                                )
                                prevd = p.add(
                                    "vector",
                                    lambda e: e.tensor_tensor(
                                        out=rbuf[:, 12:13], in0=rbuf[:, 12:13],
                                        in1=rbuf[:, 13:14], op=ALU.add),
                                    waits=[("DVE", prevd)],
                                    inc="DVE",
                                )
                            prevd = p.add(
                                "vector",
                                lambda e: e.tensor_scalar(
                                    out=rbuf[:, 14:15], in0=rbuf[:, 12:13],
                                    scalar1=2.0, scalar2=None, op0=ALU.is_lt),
                                waits=[("DVE", prevd)],
                                inc="DVE",
                            )
                            d_w = p.add(
                                "vector",
                                lambda e, ee=ee, tc=tc, bb=bb: e.tensor_tensor(
                                    out=w16[:, tc % 2, bb, ee:ee + 1],
                                    in0=rbuf[:, ee:ee + 1],
                                    in1=rbuf[:, 14:15], op=ALU.mult),
                                waits=[("DVE", prevd)],
                                inc="DVE",
                            )
                            prevd = d_w
                        w_done[tc] = p.add(
                            "scalar",
                            lambda e, tc=tc, bb=bb, ltb=ltb: e.activation(
                                out=w16[:, tc % 2, bb, 8:9], in_=ltb[:, 8:9],
                                func=AF.Sigmoid),
                            waits=[("DVE", d_w)],
                            inc="ACT",
                        )
                        tile = 4 * tc + bb
                        wv = w16[:, tc % 2, bb, 0:8]
                        pv = p.add("vector", lambda e, wv=wv: e.reduce_max(
                            out=rbuf[:, 16:17], in_=wv, axis=AXX),
                            waits=[("DVE", d_w)], inc="DVE")
                        pv = p.add("vector", lambda e, wv=wv: e.tensor_scalar(
                            out=rbuf[:, 24:32], in0=wv, scalar1=rbuf[:, 16:17],
                            scalar2=None, op0=ALU.is_ge),
                            waits=[("DVE", pv)], inc="DVE")
                        pv = p.add("vector", lambda e: e.tensor_tensor(
                            out=rbuf[:, 32:40], in0=eiota[:, :],
                            in1=rbuf[:, 24:32], op=ALU.mult),
                            waits=[("DVE", pv)], inc="DVE")
                        pv = p.add("vector", lambda e: e.tensor_scalar(
                            out=rbuf[:, 24:32], in0=rbuf[:, 24:32],
                            scalar1=-99.0, scalar2=99.0, op0=ALU.mult,
                            op1=ALU.add),
                            waits=[("DVE", pv)], inc="DVE")
                        pv = p.add("vector", lambda e: e.tensor_tensor(
                            out=rbuf[:, 24:32], in0=rbuf[:, 24:32],
                            in1=rbuf[:, 32:40], op=ALU.add),
                            waits=[("DVE", pv)], inc="DVE")
                        pv = p.add("vector", lambda e, tile=tile: e.tensor_reduce(
                            out=eT[:, tile, 0:1], in_=rbuf[:, 24:32], axis=AXX,
                            op=ALU.min),
                            waits=[("DVE", pv)], inc="DVE")
                        pv = p.add("vector", lambda e, wv=wv: e.tensor_scalar(
                            out=rbuf[:, 24:32], in0=wv, scalar1=0.0,
                            scalar2=None, op0=ALU.is_gt),
                            waits=[("DVE", pv)], inc="DVE")
                        pv = p.add("vector", lambda e: e.tensor_tensor(
                            out=rbuf[:, 32:40], in0=eiota[:, :],
                            in1=rbuf[:, 24:32], op=ALU.mult),
                            waits=[("DVE", pv)], inc="DVE")
                        pv = p.add("vector", lambda e: e.reduce_sum(
                            out=rbuf[:, 17:18], in_=rbuf[:, 32:40], axis=AXX),
                            waits=[("DVE", pv)], inc="DVE")
                        d_w = p.add("vector", lambda e, tile=tile: e.tensor_tensor(
                            out=eT[:, tile, 1:2], in0=rbuf[:, 17:18],
                            in1=eT[:, tile, 0:1], op=ALU.subtract),
                            waits=[("DVE", pv)], inc="DVE")
                    tw_end = None
                    for bb in range(4):
                        w = [("ACT", w_done[tc]), ("DVE", d_w)]
                        if tc >= 2 and bb == 0:
                            w.append(("DVE", wt_ev[tc - 2]))
                        tw_end = p.add(
                            "tensor",
                            lambda e, tc=tc, bb=bb, pb=pW[tc % 2]: e.transpose(
                                pb[0:16, 128 * bb:128 * bb + 128],
                                w16[:, tc % 2, bb, :], ident[:, :]),
                            waits=w,
                            inc="PE" if bb == 3 else None,
                        )
                    wt_ev[tc] = p.add(
                        "vector",
                        lambda e, tc=tc, pb=pW[tc % 2]: e.tensor_copy(
                            out=wT_sb[0:9, 512 * tc:512 * tc + 512],
                            in_=pb[0:9, :]),
                        waits=[("PE", tw_end)],
                        inc="DVE",
                    )
                p.dma("gpsimd", wTd.ap()[:, :], wT_sb[:, :], "CHS0",
                      waits=[("DVE", wt_ev[7])])
                pv = p.add("vector", lambda e: e.tensor_scalar(
                    out=indT[:, :], in0=wT_sb[0:8, :], scalar1=0.0,
                    scalar2=None, op0=ALU.is_gt),
                    waits=[("DVE", wt_ev[7])], inc="DVE")
                pv = p.add("vector", lambda e: e.tensor_tensor_scan(
                    out=posb[:, :], data0=indT[:, :], data1=indT[:, :],
                    initial=0.0, op0=ALU.add, op1=ALU.max),
                    waits=[("DVE", pv)], inc="DVE")
                pv = p.add("vector", lambda e: e.tensor_tensor(
                    out=posb[:, :], in0=posb[:, :], in1=indT[:, :],
                    op=ALU.subtract),
                    waits=[("DVE", pv)], inc="DVE")
                pv = p.add("vector", lambda e: e.tensor_scalar(
                    out=indT[:, :], in0=indT[:, :], scalar1=-BIGS,
                    scalar2=BIGS, op0=ALU.mult, op1=ALU.add),
                    waits=[("DVE", pv)], inc="DVE")
                d_pe = p.add("vector", lambda e: e.tensor_tensor(
                    out=posb[:, :], in0=posb[:, :], in1=indT[:, :],
                    op=ALU.add),
                    waits=[("DVE", pv)], inc="DVE")
                tev = {}
                for b in range(NTT):
                    w = [("DVE", d_pe)]
                    if b >= 2:
                        w.append(("DVE", tev[b - 2]))
                    pe_t = p.add(
                        "tensor",
                        lambda e, b=b, pb=pT2[b % 2]: e.transpose(
                            pb[:, 0:8], posb[0:8, 128 * b:128 * b + 128],
                            ident[0:8, 0:8]),
                        waits=w,
                        inc="PE",
                    )
                    pe_t2 = p.add(
                        "tensor",
                        lambda e, b=b, pb=pT2[b % 2]: e.transpose(
                            pb[:, 8:16], wT_sb[0:8, 128 * b:128 * b + 128],
                            ident[0:8, 0:8]),
                        inc="PE",
                    )
                    tev[b] = p.add(
                        "vector",
                        lambda e, b=b, pb=pT2[b % 2]: e.tensor_copy(
                            out=posT_f[:, b, :], in_=pb[:, 0:8]),
                        waits=[("PE", pe_t2)],
                        inc="DVE",
                    )
                    tev[b] = p.add(
                        "vector",
                        lambda e, b=b, pb=pT2[b % 2]: e.tensor_copy(
                            out=wTt_f[:, b, :], in_=pb[:, 8:16]),
                        inc="DVE",
                    )
                d_pti = p.add("vector", lambda e: e.tensor_copy(
                    out=posT_i[:, :, :], in_=posT_f[:, :, :]),
                    waits=[("DVE", tev[NTT - 1])], inc="DVE")
                pv = d_pti
                for tile in range(NTT):
                    for r in range(2):
                        pv = p.add("vector", lambda e, tile=tile, r=r:
                            e.tensor_scalar(
                                out=rbuf[:, 24:32], in0=eiota[:, :],
                                scalar1=eT[:, tile, r:r + 1], scalar2=None,
                                op0=ALU.is_equal),
                            waits=[("DVE", pv)], inc="DVE")
                        pv = p.add("vector", lambda e, tile=tile: e.tensor_tensor(
                            out=rbuf[:, 32:40], in0=posT_f[:, tile, :],
                            in1=rbuf[:, 24:32], op=ALU.mult),
                            waits=[("DVE", pv)], inc="DVE")
                        pv = p.add("vector", lambda e: e.reduce_sum(
                            out=rbuf[:, 16:17], in_=rbuf[:, 32:40], axis=AXX),
                            waits=[("DVE", pv)], inc="DVE")
                        pv = p.add("vector", lambda e: e.tensor_scalar(
                            out=rbuf[:, 18:19], in0=rbuf[:, 16:17],
                            scalar1=CAP - 0.5, scalar2=None, op0=ALU.is_gt),
                            waits=[("DVE", pv)], inc="DVE")
                        pv = p.add("vector", lambda e, tile=tile, r=r:
                            e.scalar_tensor_tensor(
                                out=rbuf[:, 19:20], in0=eT[:, tile, r:r + 1],
                                scalar=float(CAP), in1=rbuf[:, 16:17],
                                op0=ALU.mult, op1=ALU.add),
                            waits=[("DVE", pv)], inc="DVE")
                        pv = p.add("vector", lambda e, tile=tile, r=r:
                            e.scalar_tensor_tensor(
                                out=gidx_f[:, tile, r:r + 1],
                                in0=rbuf[:, 18:19], scalar=BIGS,
                                in1=rbuf[:, 19:20], op0=ALU.mult, op1=ALU.add),
                            waits=[("DVE", pv)], inc="DVE")
                d_gi = p.add("vector", lambda e: e.tensor_copy(
                    out=gidx_i[:, :, :], in_=gidx_f[:, :, :]),
                    waits=[("DVE", pv)], inc="DVE")
                p.add("gpsimd", lambda e: e.nop(), waits=p.st_waits())
                p.run()

            with contextlib.ExitStack() as ps:
                wgraw = ps.enter_context(nc.sbuf_tensor("wgraw", [128, 2, D], F32))
                wuraw = ps.enter_context(nc.sbuf_tensor("wuraw", [128, 2, D], F32))
                wgTh2 = ps.enter_context(
                    nc.sbuf_tensor("wgTh2", [128, 2, ND, 128], BF16))
                wgTl2 = ps.enter_context(
                    nc.sbuf_tensor("wgTl2", [128, 2, ND, 128], BF16))
                wuTh2 = ps.enter_context(
                    nc.sbuf_tensor("wuTh2", [128, 2, ND, 128], BF16))
                wuTl2 = ps.enter_context(
                    nc.sbuf_tensor("wuTl2", [128, 2, ND, 128], BF16))
                wb = ps.enter_context(nc.sbuf_tensor("wb", [128, T], F32))
                t1 = ps.enter_context(nc.sbuf_tensor("t1", [128, 1, 512], F32))
                t2 = ps.enter_context(nc.sbuf_tensor("t2", [128, 1, 512], F32))
                t2h = ps.enter_context(nc.sbuf_tensor("t2h", [128, 2, 512], BF16))
                t2l = ps.enter_context(nc.sbuf_tensor("t2l", [128, 2, 512], BF16))
                h2t = ps.enter_context(nc.sbuf_tensor("h2t", [128, 2, D], F32))
                xwt = ps.enter_context(nc.sbuf_tensor("xwt", [128, 2, D], F32))
                pWT = [ps.enter_context(nc.psum_tensor(f"p6w{i}", [128, 512], F32))
                       for i in range(2)]
                pg = [ps.enter_context(nc.psum_tensor(f"p6g{i}", [128, 512], F32))
                      for i in range(2)]
                pu = [ps.enter_context(nc.psum_tensor(f"p6u{i}", [128, 512], F32))
                      for i in range(2)]
                p = Phase(nc, sems, cnt)
                ldwb = p.dma("sync", wb[:, :],
                             dap(wTd, 8 * T, (0, 128), (1, T)), "CHC")
                sc_steps = _scatter_steps(p, range(0, 4), h2t, xwt, h2tmd,
                                          xar, xwar, posT_i, wTt_f, d_pti,
                                          d_gi)
                sc_it = iter(sc_steps)
                mm_end = {}
                silu_a = {}
                dh1 = {}
                hid1 = {}
                st_done = {}
                wt_ev = {}
                for i in range(22):
                    w = [("PE", mm_end[(i - 2, 7)])] if i >= 2 else []
                    srcg = wg[128 * i:128 * i + 128, :]
                    srcu = wu[128 * i:128 * i + 128, :]
                    p.dma("sync", wgraw[:, i % 2, :], srcg,
                          f"CHL{i % 2}", waits=w)
                    ldu = p.dma("sync", wuraw[:, i % 2, :], srcu,
                                f"CHL{i % 2}", waits=w)
                    for k in range(4):
                        raw = (wgraw if k < 2 else wuraw)[:, i % 2, :]
                        kg = i * 4 + k
                        t_end = None
                        for dd in range(4):
                            d = 4 * (k % 2) + dd
                            w = [ldu]
                            if kg >= 2 and dd == 0:
                                w.append(("DVE", wt_ev[kg - 2]))
                            t_end = p.add(
                                "tensor",
                                lambda e, raw=raw, d=d, dd=dd, pb=pWT[kg % 2]:
                                    e.transpose(
                                        pb[:, 128 * dd:128 * dd + 128],
                                        raw[:, 128 * d:128 * d + 128],
                                        ident[:, :]),
                                waits=w,
                                inc="PE" if dd == 3 else None,
                            )
                        dsth2 = (wgTh2 if k < 2 else wuTh2)
                        dstl2 = (wgTl2 if k < 2 else wuTl2)
                        half = k % 2
                        w = [("PE", t_end)]
                        if i >= 2:
                            w.append(("PE", mm_end[(i - 2, 7)]))
                        s_hi = p.add(
                            "scalar",
                            lambda e, dstT=dsth2, i=i, half=half, pb=pWT[kg % 2]:
                                e.copy(
                                    out=dstT[:, i % 2, 4 * half:4 * half + 4, :],
                                    in_=pb[:, :].rearrange(
                                        "p (a b) -> p a b", a=4)),
                            waits=w,
                            inc="ACT",
                        )
                        wt_ev[kg] = p.add(
                            "vector",
                            lambda e, dh2=dsth2, dl2=dstl2, i=i, half=half,
                                   pb=pWT[kg % 2]:
                                e.scalar_tensor_tensor(
                                    out=dl2[:, i % 2, 4 * half:4 * half + 4, :],
                                    in0=pb[:, :].rearrange(
                                        "p (a b) -> p a b", a=4),
                                    scalar=1.0,
                                    in1=dh2[:, i % 2, 4 * half:4 * half + 4, :],
                                    op0=ALU.mult, op1=ALU.subtract),
                            waits=[("ACT", s_hi)],
                            inc="DVE",
                        )
                    wt_last = wt_ev[i * 4 + 3]
                    for c in range(8):
                        gc = i * 8 + c
                        np3 = 0
                        for (wa, hb) in ((0, 0), (0, 1), (1, 0)):
                            wsrc = (wgTh2 if wa == 0 else wgTl2)
                            hsrc = (h2h if hb == 0 else h2l)
                            for d in range(ND):
                                w = [("DVE", wt_last)]
                                if gc >= 2 and np3 == 0:
                                    w.append(("ACT", silu_a.get(gc - 2)))
                                p.add(
                                    "tensor",
                                    lambda e, i=i, d=d, c=c, ws=wsrc, hs=hsrc,
                                           st=(np3 == 0), sp=(np3 == 23),
                                           pb=pg[c % 2]: e.matmul(
                                        pb[:, :], ws[:, i % 2, d, :],
                                        hs[:, d, 512 * c:512 * c + 512],
                                        start=st, stop=sp),
                                    waits=w,
                                )
                                np3 += 1
                        np3 = 0
                        for (wa, hb) in ((0, 0), (0, 1), (1, 0)):
                            wsrc = (wuTh2 if wa == 0 else wuTl2)
                            hsrc = (h2h if hb == 0 else h2l)
                            for d in range(ND):
                                w = []
                                if gc >= 2 and np3 == 0:
                                    w.append(("DVE", dh1.get(gc - 2)))
                                mm_end[(i, c)] = p.add(
                                    "tensor",
                                    lambda e, i=i, d=d, c=c, ws=wsrc, hs=hsrc,
                                           st=(np3 == 0), sp=(np3 == 23),
                                           pb=pu[c % 2]: e.matmul(
                                        pb[:, :], ws[:, i % 2, d, :],
                                        hs[:, d, 512 * c:512 * c + 512],
                                        start=st, stop=sp),
                                    waits=w,
                                    inc="PE" if np3 == 23 else None,
                                )
                                np3 += 1
                        w = [("PE", mm_end[(i, c)])]
                        if gc >= 1:
                            w.append(("DVE", dh1.get(gc - 1)))
                        silu_a[gc] = p.add(
                            "scalar",
                            lambda e, c=c, pb=pg[c % 2]: e.activation(
                                out=t1[:, 0, :], in_=pb[:, :],
                                func=AF.Silu),
                            waits=w,
                            inc="ACT",
                        )
                        w = [("ACT", silu_a[gc])]
                        if gc >= 1:
                            w.append(("DVE", hid1.get(gc - 1)))
                        if gc >= 2:
                            w.append(st_done.get(gc - 2))
                        dh1[gc] = p.add(
                            "vector",
                            lambda e, c=c, pb=pu[c % 2]: e.tensor_tensor(
                                out=t2[:, 0, :], in0=t1[:, 0, :],
                                in1=pb[:, :], op=ALU.mult),
                            waits=w,
                            inc="DVE",
                        )
                        w = [("DVE", dh1[gc]), ldwb]
                        hid0 = p.add(
                            "vector",
                            lambda e, c=c: e.tensor_tensor(
                                out=t2[:, 0, :], in0=t2[:, 0, :],
                                in1=wb[:, 512 * c:512 * c + 512],
                                op=ALU.mult),
                            waits=w,
                            inc="DVE",
                        )
                        hhi = p.add(
                            "scalar",
                            lambda e, gc=gc: e.copy(
                                out=t2h[:, gc % 2, :], in_=t2[:, 0, :]),
                            waits=[("DVE", hid0)] + (
                                [st_done[gc - 2]] if gc >= 2 else []),
                            inc="ACT",
                        )
                        hid1[gc] = p.add(
                            "vector",
                            lambda e, c=c, gc=gc: e.tensor_tensor(
                                out=t2l[:, gc % 2, :], in0=t2[:, 0, :],
                                in1=t2h[:, gc % 2, :], op=ALU.subtract),
                            waits=[("ACT", hhi)],
                            inc="DVE",
                        )
                        p.dma(
                            "gpsimd",
                            dap(hidd_sh, (128 * i) * T + 512 * c,
                                (T, 128), (1, 512)),
                            t2h[:, gc % 2, :], f"CHS{gc % 2}",
                            waits=[("ACT", hhi), ("DVE", hid1[gc])])
                        st_done[gc] = p.dma(
                            "gpsimd",
                            dap(hidd_sl, (128 * i) * T + 512 * c,
                                (T, 128), (1, 512)),
                            t2l[:, gc % 2, :], f"CHS{gc % 2}",
                            waits=[("DVE", hid1[gc])])
                        if gc % 2 == 0:
                            nxt = next(sc_it, None)
                            if nxt:
                                nxt()
                for nxt in sc_it:
                    nxt()
                p.add("gpsimd", lambda e: e.nop(),
                      waits=p.st_waits() + [(f"CHX{e2}", cnt[f"CHX{e2}"])
                                            for e2 in range(0, 4)])
                p.run()

        with contextlib.ExitStack() as ps:
            hidch = ps.enter_context(
                nc.sbuf_tensor("hidcBh", [128, 2, 22, 128], BF16))
            hidcl = ps.enter_context(
                nc.sbuf_tensor("hidcBl", [128, 2, 22, 128], BF16))
            draw = ps.enter_context(nc.sbuf_tensor("drawB", [128, 2, FS], F32))
            wdTh = ps.enter_context(nc.sbuf_tensor("wdTBh", [128, 22, D], BF16))
            wdTl = ps.enter_context(nc.sbuf_tensor("wdTBl", [128, 22, D], BF16))
            dstage = ps.enter_context(nc.sbuf_tensor("dstageB", [128, 2, D], F32))
            h2t = ps.enter_context(nc.sbuf_tensor("h2tB", [128, 2, D], F32))
            xwt = ps.enter_context(nc.sbuf_tensor("xwtB", [128, 2, D], F32))
            pWT = [ps.enter_context(nc.psum_tensor(f"p8w{i}", [128, 512], F32))
                   for i in range(2)]
            pD = [ps.enter_context(nc.psum_tensor(f"p8d{i}", [128, 512], F32))
                  for i in range(2)]
            p = Phase(nc, sems, cnt)
            sc_steps = _scatter_steps(p, range(4, 8), h2t, xwt, h2tmd,
                                      xar, xwar, posT_i, wTt_f, d_pti, d_gi)
            sc_it = iter(sc_steps)
            wt_ev = {}
            hh_t_end = {}
            kidx = 0
            for hh in range(ND):
                w = []
                if hh >= 2:
                    w.append(("PE", hh_t_end[hh - 2]))
                ldh = p.dma("sync", draw[:, hh % 2, :],
                            wd[128 * hh:128 * hh + 128, :],
                            f"CHL{hh % 2}", waits=w)
                for kb in range(6):
                    t_end = None
                    f0 = 4 * kb
                    nfk = min(4, 22 - f0)
                    for dd in range(nfk):
                        f = f0 + dd
                        w = [ldh]
                        if kidx >= 2 and dd == 0:
                            w.append(("DVE", wt_ev[kidx - 2]))
                        t_end = p.add(
                            "tensor",
                            lambda e, hh=hh, f=f, dd=dd, pb=pWT[kidx % 2]:
                                e.transpose(pb[:, 128 * dd:128 * dd + 128],
                                            draw[:, hh % 2,
                                                 128 * f:128 * f + 128],
                                            ident[:, :]),
                            waits=w,
                            inc="PE" if dd == nfk - 1 else None,
                        )
                    hh_t_end[hh] = t_end
                    a_hi = p.add(
                        "scalar",
                        lambda e, hh=hh, f0=f0, nfk=nfk, pb=pWT[kidx % 2]:
                            e.copy(
                                out=wdTh[:, f0:f0 + nfk,
                                         128 * hh:128 * hh + 128],
                                in_=pb[:, 0:nfk * 128].rearrange(
                                    "p (a b) -> p a b", a=nfk)),
                        waits=[("PE", t_end)],
                        inc="ACT",
                    )
                    wt_ev[kidx] = p.add(
                        "vector",
                        lambda e, hh=hh, f0=f0, nfk=nfk, pb=pWT[kidx % 2]:
                            e.scalar_tensor_tensor(
                                out=wdTl[:, f0:f0 + nfk,
                                         128 * hh:128 * hh + 128],
                                in0=pb[:, 0:nfk * 128].rearrange(
                                    "p (a b) -> p a b", a=nfk),
                                scalar=1.0,
                                in1=wdTh[:, f0:f0 + nfk,
                                         128 * hh:128 * hh + 128],
                                op0=ALU.mult, op1=ALU.subtract),
                        waits=[("ACT", a_hi)],
                        inc="DVE",
                    )
                    kidx += 1
            prep_done = wt_ev[kidx - 1]
            mm_end = {}
            dv_ev = {}
            st_done = {}
            for m in range(NTT):
                w = []
                if m >= 2:
                    w.append(("PE", mm_end[(m - 2, 1)]))
                p.dma(
                    "sync", hidch[:, m % 2, :, :],
                    dap(hidd_sh, 128 * m, (T, 128), (128 * T, 22), (1, 128)),
                    f"CHL{2 + m % 2}", waits=w)
                ldc = p.dma(
                    "sync", hidcl[:, m % 2, :, :],
                    dap(hidd_sl, 128 * m, (T, 128), (128 * T, 22), (1, 128)),
                    f"CHL{2 + m % 2}", waits=w)
                for hc in range(2):
                    gmc = 2 * m + hc
                    np3 = 0
                    for (ha, hb) in ((0, 0), (0, 1), (1, 0)):
                        for f in range(22):
                            w = [ldc, ("DVE", prep_done)]
                            if gmc >= 2 and np3 == 0:
                                idx = gmc - 2
                                w.append(("DVE", dv_ev[(idx // 2, idx % 2)]))
                            hsrc2 = (hidch if ha == 0 else hidcl)
                            wsrc2 = (wdTh if hb == 0 else wdTl)
                            mm_end[(m, hc)] = p.add(
                                "tensor",
                                lambda e, m=m, hc=hc, f=f, hs=hsrc2, ws=wsrc2,
                                       st=(np3 == 0), pb=pD[gmc % 2]:
                                    e.matmul(pb[:, :], hs[:, m % 2, f, :],
                                             ws[:, f, 512 * hc:512 * hc + 512],
                                             start=st, stop=False),
                                waits=w,
                                inc=None,
                            )
                            np3 += 1
                    mm_end[(m, hc)] = p.add(
                        "tensor",
                        lambda e, m=m, hc=hc, pb=pD[gmc % 2]:
                            e.matmul(pb[:, :], hidcl[:, m % 2, 21, :],
                                     wdTl[:, 21, 512 * hc:512 * hc + 512],
                                     start=False, stop=True),
                        inc="PE",
                    )
                    w = [("PE", mm_end[(m, hc)])]
                    if gmc >= 2:
                        idx = gmc - 2
                        w.append(st_done[(idx // 2, idx % 2)])
                    dv_ev[(m, hc)] = p.add(
                        "vector",
                        lambda e, m=m, hc=hc, pb=pD[gmc % 2]: e.tensor_copy(
                            out=dstage[:, m % 2, 512 * hc:512 * hc + 512],
                            in_=pb[:, :]),
                        waits=w,
                        inc="DVE",
                    )
                    st_done[(m, hc)] = p.dma(
                        "gpsimd",
                        shoutd.ap()[128 * m:128 * m + 128,
                                    512 * hc:512 * hc + 512],
                        dstage[:, m % 2, 512 * hc:512 * hc + 512],
                        f"CHS{gmc % 2}",
                        waits=[("DVE", dv_ev[(m, hc)])])
                    nxt = next(sc_it, None)
                    if nxt:
                        nxt()
            for nxt in sc_it:
                nxt()
            p.add("gpsimd", lambda e: e.nop(),
                  waits=p.st_waits() + [(f"CHX{e2}", cnt[f"CHX{e2}"])
                                        for e2 in range(4, 8)])
            p.run()

        with contextlib.ExitStack() as ps:
            ast_x = ps.enter_context(nc.sbuf_tensor("ast_x", [128, D], F32))
            ast_w = ps.enter_context(nc.sbuf_tensor("ast_w", [128, D], F32))
            xfmh = ps.enter_context(nc.sbuf_tensor("xfmh", [128, ND, 512], BF16))
            xfml = ps.enter_context(nc.sbuf_tensor("xfml", [128, ND, 512], BF16))
            xwfmh = ps.enter_context(nc.sbuf_tensor("xwfmh", [128, ND, 512], BF16))
            xwfml = ps.enter_context(nc.sbuf_tensor("xwfml", [128, ND, 512], BF16))
            wgTh = ps.enter_context(nc.sbuf_tensor("wgTh", [128, 2, ND, 128], BF16))
            wgTl = ps.enter_context(nc.sbuf_tensor("wgTl", [128, 2, ND, 128], BF16))
            wuTh = ps.enter_context(nc.sbuf_tensor("wuTh", [128, 2, ND, 128], BF16))
            wuTl = ps.enter_context(nc.sbuf_tensor("wuTl", [128, 2, ND, 128], BF16))
            wgraw = ps.enter_context(nc.sbuf_tensor("wgrawe", [128, 2, D], F32))
            wuraw = ps.enter_context(nc.sbuf_tensor("wurawe", [128, 2, D], F32))
            t1e = ps.enter_context(nc.sbuf_tensor("t1e", [128, 2, 512], F32))
            t2e = ps.enter_context(nc.sbuf_tensor("t2e", [128, 2, 512], F32))
            hidqh = ps.enter_context(nc.sbuf_tensor("hidqh", [128, 2, 11, 512], BF16))
            hidql = ps.enter_context(nc.sbuf_tensor("hidql", [128, 2, 11, 512], BF16))
            wdTeh = ps.enter_context(nc.sbuf_tensor("wdTeh", [128, 11, D], BF16))
            wdTel = ps.enter_context(nc.sbuf_tensor("wdTel", [128, 11, D], BF16))
            drawe = ps.enter_context(nc.sbuf_tensor("drawe", [128, FE], F32))
            dst2 = ps.enter_context(nc.sbuf_tensor("dst2", [128, 2, 512], F32))
            pWT = [ps.enter_context(nc.psum_tensor(f"pew{i}", [128, 512], F32))
                   for i in range(2)]
            pg = [ps.enter_context(nc.psum_tensor(f"peg{i}", [128, 512], F32))
                  for i in range(2)]
            pu = [ps.enter_context(nc.psum_tensor(f"peu{i}", [128, 512], F32))
                  for i in range(2)]
            pD = [ps.enter_context(nc.psum_tensor(f"ped{i}", [128, 512], F32))
                  for i in range(2)]
            for ee in range(NE):
                p = Phase(nc, sems, cnt)
                scw = (f"CHX{ee}", cnt[f"CHX{ee}"])

                def _split_evac(pb, nfk, out_h, out_l, t_end, extra=()):
                    a_hi = p.add(
                        "scalar",
                        lambda e, pb=pb, nfk=nfk, out_h=out_h: e.copy(
                            out=out_h,
                            in_=pb[:, 0:nfk * 128].rearrange(
                                "p (a b) -> p a b", a=nfk)),
                        waits=[("PE", t_end)] + list(extra),
                        inc="ACT",
                    )
                    return p.add(
                        "vector",
                        lambda e, pb=pb, nfk=nfk, out_h=out_h, out_l=out_l:
                            e.scalar_tensor_tensor(
                                out=out_l,
                                in0=pb[:, 0:nfk * 128].rearrange(
                                    "p (a b) -> p a b", a=nfk),
                                scalar=1.0, in1=out_h,
                                op0=ALU.mult, op1=ALU.subtract),
                        waits=[("ACT", a_hi)],
                        inc="DVE",
                    )

                wt_ev = {}
                hh_t_end = {}
                kidx = 0
                for hh in range(ND):
                    w = []
                    if hh >= 1:
                        w.append(("PE", hh_t_end[hh - 1]))
                    ldh = p.dma("sync", drawe[:, :],
                                edown[ee, 128 * hh:128 * hh + 128, :],
                                f"CHL{hh % 2}", waits=w)
                    for kb in range(3):
                        t_end = None
                        f0 = 4 * kb
                        nfk = min(4, 11 - f0)
                        for dd in range(nfk):
                            f = f0 + dd
                            w = [ldh]
                            if kidx >= 2 and dd == 0:
                                w.append(("DVE", wt_ev[kidx - 2]))
                            t_end = p.add(
                                "tensor",
                                lambda e, f=f, dd=dd, pb=pWT[kidx % 2]:
                                    e.transpose(pb[:, 128 * dd:128 * dd + 128],
                                                drawe[:, 128 * f:128 * f + 128],
                                                ident[:, :]),
                                waits=w,
                                inc="PE" if dd == nfk - 1 else None,
                            )
                        hh_t_end[hh] = t_end
                        wt_ev[kidx] = _split_evac(
                            pWT[kidx % 2], nfk,
                            wdTeh[:, f0:f0 + nfk, 128 * hh:128 * hh + 128],
                            wdTel[:, f0:f0 + nfk, 128 * hh:128 * hh + 128],
                            t_end)
                        kidx += 1
                dprep = wt_ev[kidx - 1]
                pwt_hist = dict(wt_ev)
                pwt_last_ev = dprep
                pwt_kidx = kidx
                silu_a = {}
                dh1 = {}
                hid_last = {}
                mm_gu = {}
                mm_d = {}
                dv_ev = {}
                st_done = {}
                gu_end = {}
                at_end = {}
                aev = {}
                for c in range(NSC):
                    for q in range(4):
                        qg = 4 * c + q
                        base = ee * CAP + 512 * c + 128 * q
                        w = [scw]
                        if qg >= 1:
                            w.append(("PE", at_end[qg - 1]))
                        if q == 0 and c >= 1:
                            w.append(("PE", gu_end[c - 1]))
                        lax = p.dma("sync", ast_x[:, :],
                                    dap(xar, base * D, (D, 128), (1, D)),
                                    f"CHL{2 + qg % 2}", waits=w)
                        law = p.dma("sync", ast_w[:, :],
                                    dap(xwar, base * D, (D, 128), (1, D)),
                                    f"CHL{2 + qg % 2}", waits=w)
                        for half in range(2):
                            for src_i in range(2):
                                t_end = None
                                for dd in range(4):
                                    d = 4 * half + dd
                                    buf = ast_x if src_i == 0 else ast_w
                                    w = [law]
                                    if dd == 0:
                                        w.append(("DVE",
                                                  pwt_hist.get(pwt_kidx - 2)))
                                    t_end = p.add(
                                        "tensor",
                                        lambda e, buf=buf, d=d, dd=dd,
                                               pb=pWT[pwt_kidx % 2]:
                                            e.transpose(
                                                pb[:, 128 * dd:128 * dd + 128],
                                                buf[:, 128 * d:128 * d + 128],
                                                ident[:, :]),
                                        waits=w,
                                        inc="PE" if dd == 3 else None,
                                    )
                                dsth = (xfmh if src_i == 0 else xwfmh)
                                dstl = (xfml if src_i == 0 else xwfml)
                                ev = _split_evac(
                                    pWT[pwt_kidx % 2], 4,
                                    dsth[:, 4 * half:4 * half + 4,
                                         128 * q:128 * q + 128],
                                    dstl[:, 4 * half:4 * half + 4,
                                         128 * q:128 * q + 128],
                                    t_end)
                                pwt_hist[pwt_kidx] = ev
                                pwt_last_ev = ev
                                pwt_kidx += 1
                        at_end[qg] = t_end
                        aev[qg] = pwt_last_ev
                    a_last = aev[4 * c + 3]
                    for i in range(11):
                        ig = 11 * c + i
                        w = []
                        if i >= 2:
                            w.append(("PE", mm_gu[(c, i - 2)]))
                        elif c >= 1:
                            w.append(("PE", mm_gu[(c - 1, 9 + i)]))
                        p.dma("sync", wgraw[:, i % 2, :],
                              egu[ee, 128 * i:128 * i + 128, :],
                              f"CHL{i % 2}", waits=w)
                        ldu = p.dma("sync", wuraw[:, i % 2, :],
                                    egu[ee, FE + 128 * i:FE + 128 * i + 128, :],
                                    f"CHL{i % 2}", waits=w)
                        for k in range(4):
                            raw = (wgraw if k < 2 else wuraw)[:, i % 2, :]
                            t_end = None
                            for dd in range(4):
                                d = 4 * (k % 2) + dd
                                w = [ldu]
                                if dd == 0:
                                    w.append(("DVE",
                                              pwt_hist.get(pwt_kidx - 2)))
                                t_end = p.add(
                                    "tensor",
                                    lambda e, raw=raw, d=d, dd=dd,
                                           pb=pWT[pwt_kidx % 2]:
                                        e.transpose(
                                            pb[:, 128 * dd:128 * dd + 128],
                                            raw[:, 128 * d:128 * d + 128],
                                            ident[:, :]),
                                    waits=w,
                                    inc="PE" if dd == 3 else None,
                                )
                            half = k % 2
                            dsth = (wgTh if k < 2 else wuTh)
                            dstl = (wgTl if k < 2 else wuTl)
                            w2 = []
                            if i >= 2:
                                w2.append(("PE", mm_gu[(c, i - 2)]))
                            elif c >= 1:
                                w2.append(("PE", mm_gu[(c - 1, 9 + i)]))
                            ev = _split_evac(
                                pWT[pwt_kidx % 2], 4,
                                dsth[:, i % 2, 4 * half:4 * half + 4, :],
                                dstl[:, i % 2, 4 * half:4 * half + 4, :],
                                t_end, extra=w2)
                            pwt_hist[pwt_kidx] = ev
                            pwt_last_ev = ev
                            pwt_kidx += 1
                        wt_last = pwt_last_ev
                        np3 = 0
                        for (wa, xa2) in ((0, 0), (0, 1), (1, 0)):
                            wsrc = (wgTh if wa == 0 else wgTl)
                            xsrc = (xfmh if xa2 == 0 else xfml)
                            for d in range(ND):
                                w = [("DVE", wt_last), ("DVE", a_last)]
                                if ig >= 2 and np3 == 0:
                                    w.append(("ACT", silu_a.get(ig - 2)))
                                p.add(
                                    "tensor",
                                    lambda e, i=i, d=d, ws=wsrc, xs=xsrc,
                                           st=(np3 == 0), sp=(np3 == 23),
                                           pb=pg[i % 2]: e.matmul(
                                        pb[:, :], ws[:, i % 2, d, :],
                                        xs[:, d, :], start=st, stop=sp),
                                    waits=w,
                                )
                                np3 += 1
                        np3 = 0
                        for (wa, xa2) in ((0, 0), (0, 1), (1, 0)):
                            wsrc = (wuTh if wa == 0 else wuTl)
                            xsrc = (xwfmh if xa2 == 0 else xwfml)
                            for d in range(ND):
                                w = []
                                if ig >= 2 and np3 == 0:
                                    w.append(("DVE", dh1.get(ig - 2)))
                                mm_gu[(c, i)] = p.add(
                                    "tensor",
                                    lambda e, i=i, d=d, ws=wsrc, xs=xsrc,
                                           st=(np3 == 0), sp=(np3 == 23),
                                           pb=pu[i % 2]: e.matmul(
                                        pb[:, :], ws[:, i % 2, d, :],
                                        xs[:, d, :], start=st, stop=sp),
                                    waits=w,
                                    inc="PE" if np3 == 23 else None,
                                )
                                np3 += 1
                        w = [("PE", mm_gu[(c, i)])]
                        if ig >= 2:
                            w.append(("DVE", dh1.get(ig - 2)))
                        silu_a[ig] = p.add(
                            "scalar",
                            lambda e, i=i, pb=pg[i % 2]: e.activation(
                                out=t1e[:, i % 2, :], in_=pb[:, :],
                                func=AF.Silu),
                            waits=w,
                            inc="ACT",
                        )
                        w = [("ACT", silu_a[ig])]
                        if c >= 2:
                            w.append(("PE", mm_d[(c - 2, 7)]))
                        dmul = p.add(
                            "vector",
                            lambda e, i=i, pb=pu[i % 2]: e.tensor_tensor(
                                out=t2e[:, i % 2, :], in0=t1e[:, i % 2, :],
                                in1=pb[:, :], op=ALU.mult),
                            waits=w,
                            inc="DVE",
                        )
                        h_hi = p.add(
                            "scalar",
                            lambda e, c=c, i=i: e.copy(
                                out=hidqh[:, c % 2, i, :],
                                in_=t2e[:, i % 2, :]),
                            waits=[("DVE", dmul)],
                            inc="ACT",
                        )
                        dh1[ig] = p.add(
                            "vector",
                            lambda e, c=c, i=i: e.tensor_tensor(
                                out=hidql[:, c % 2, i, :],
                                in0=t2e[:, i % 2, :],
                                in1=hidqh[:, c % 2, i, :], op=ALU.subtract),
                            waits=[("ACT", h_hi)],
                            inc="DVE",
                        )
                    hid_last[c] = dh1[11 * c + 10]
                    gu_end[c] = mm_gu[(c, 10)]
                    for mq in range(4):
                        for hc in range(2):
                            gmc = 8 * c + 2 * mq + hc
                            np3 = 0
                            for (ha, hb) in ((0, 0), (0, 1), (1, 0), (1, 1)):
                                hsrc = (hidqh if ha == 0 else hidql)
                                wsrc = (wdTeh if hb == 0 else wdTel)
                                for f in range(11):
                                    w = [("DVE", hid_last[c]),
                                         ("DVE", dprep)]
                                    if gmc >= 2 and np3 == 0:
                                        w.append(("DVE", dv_ev.get(gmc - 2)))
                                    mm_d[(c, 2 * mq + hc)] = p.add(
                                        "tensor",
                                        lambda e, c=c, mq=mq, hc=hc, f=f,
                                               hs=hsrc, ws=wsrc,
                                               st=(np3 == 0), sp=(np3 == 43),
                                               pb=pD[gmc % 2]:
                                            e.matmul(
                                                pb[:, :],
                                                hs[:, c % 2, f,
                                                   128 * mq:128 * mq + 128],
                                                ws[:, f,
                                                   512 * hc:512 * hc + 512],
                                                start=st, stop=sp),
                                        waits=w,
                                        inc="PE" if np3 == 43 else None,
                                    )
                                    np3 += 1
                            w = [("PE", mm_d[(c, 2 * mq + hc)])]
                            if gmc >= 2:
                                w.append(st_done.get(gmc - 2))
                            dv_ev[gmc] = p.add(
                                "vector",
                                lambda e, gmc=gmc, pb=pD[gmc % 2]:
                                    e.tensor_copy(
                                        out=dst2[:, gmc % 2, :], in_=pb[:, :]),
                                waits=w,
                                inc="DVE",
                            )
                            st_done[gmc] = p.dma(
                                "gpsimd",
                                dap(dar,
                                    (ee * CAP + 512 * c + 128 * mq) * D
                                    + 512 * hc,
                                    (D, 128), (1, 512)),
                                dst2[:, gmc % 2, :], f"CHS{gmc % 2}",
                                waits=[("DVE", dv_ev[gmc])])
                p.add("gpsimd", lambda e: e.nop(), waits=p.st_waits())
                p.run()

        with contextlib.ExitStack() as ps:
            x1b = ps.enter_context(nc.sbuf_tensor("x1bF", [128, 2, D], F32))
            shb = ps.enter_context(nc.sbuf_tensor("shbF", [128, 2, D], F32))
            g0b = ps.enter_context(nc.sbuf_tensor("g0bF", [128, 2, D], F32))
            g1b = ps.enter_context(nc.sbuf_tensor("g1bF", [128, 2, D], F32))
            acc = ps.enter_context(nc.sbuf_tensor("accF", [128, 2, D], F32))
            p = Phase(nc, sems, cnt)
            holdF = {}

            def _setupF(e):
                bc = e.alloc_register()
                e.reg_mov(bc, NE * CAP - 1)
                holdF["bc"] = bc
                return e.nop()

            p.add("gpsimd", _setupF)
            a1 = {}
            a3 = {}
            g_done = {}
            st_out = {}
            for tile in range(NTT):
                w = [("DVE", a1.get(tile - 2))] if tile >= 2 else []
                ldx1 = p.dma("sync", x1b[:, tile % 2, :],
                             x1d.ap()[128 * tile:128 * tile + 128, :],
                             f"CHL{tile % 2}", waits=w)
                ldsh = p.dma("sync", shb[:, tile % 2, :],
                             shoutd.ap()[128 * tile:128 * tile + 128, :],
                             f"CHL{2 + tile % 2}", waits=w)
                w2 = [("DVE", a3.get(tile - 2))] if tile >= 2 else []
                g0 = p.dma(
                    "gpsimd", g0b[:, tile % 2, :], None, f"CHX{tile % 8}",
                    waits=w2) if False else p.add(
                    "gpsimd",
                    lambda e, tile=tile: e.indirect_dma_start(
                        out=g0b[:, tile % 2, :],
                        out_offset=None,
                        in_=dar.ap(),
                        in_offset=bass.IndirectOffsetOnAxis(
                            gidx_i[:, tile, 0:1], 0),
                        bounds_check=holdF["bc"],
                        oob_is_err=False),
                    waits=w2,
                    inc=f"CHX{tile % 8}",
                    inc_n=16,
                )
                g1 = p.add(
                    "gpsimd",
                    lambda e, tile=tile: e.indirect_dma_start(
                        out=g1b[:, tile % 2, :],
                        out_offset=None,
                        in_=dar.ap(),
                        in_offset=bass.IndirectOffsetOnAxis(
                            gidx_i[:, tile, 1:2], 0),
                        bounds_check=holdF["bc"],
                        oob_is_err=False),
                    inc=f"CHX{tile % 8}",
                    inc_n=16,
                )
                g_done[tile] = (f"CHX{tile % 8}", g1)
                a1[tile] = p.add(
                    "vector",
                    lambda e, tile=tile: e.tensor_tensor(
                        out=acc[:, tile % 2, :], in0=x1b[:, tile % 2, :],
                        in1=shb[:, tile % 2, :], op=ALU.add),
                    waits=[ldx1, ldsh] + ([st_out.get(tile - 2)]
                                          if tile >= 2 else []),
                    inc="DVE",
                )
                a2 = p.add(
                    "vector",
                    lambda e, tile=tile: e.tensor_tensor(
                        out=acc[:, tile % 2, :], in0=acc[:, tile % 2, :],
                        in1=g0b[:, tile % 2, :], op=ALU.add),
                    waits=[("DVE", a1[tile]), g_done[tile]],
                    inc="DVE",
                )
                a3[tile] = p.add(
                    "vector",
                    lambda e, tile=tile: e.tensor_tensor(
                        out=acc[:, tile % 2, :], in0=acc[:, tile % 2, :],
                        in1=g1b[:, tile % 2, :], op=ALU.add),
                    waits=[("DVE", a2)],
                    inc="DVE",
                )
                st_out[tile] = p.dma(
                    "gpsimd",
                    out_t[128 * tile:128 * tile + 128, :],
                    acc[:, tile % 2, :], f"CHS{tile % 2}",
                    waits=[("DVE", a3[tile])])
            p.add("gpsimd", lambda e: e.nop(), waits=p.st_waits())
            p.run()

    return nc


_NC_CACHE = [None]


def kernel(L0_attn_x, L0_attn_W_qkv, L0_attn_b_qkv, L0_ffn_gate_w,
           L0_ffn_experts_gate_up, L0_ffn_experts_down, L0_ffn_shared_gate_w,
           L0_ffn_shared_up_w, L0_ffn_shared_down_w, L0_ffn_shared_expert_gate_w):
    x = np.ascontiguousarray(np.asarray(L0_attn_x, dtype=np.float32))
    weights = {
        "wqkv": np.ascontiguousarray(np.asarray(L0_attn_W_qkv, dtype=np.float32)),
        "bqkv": np.ascontiguousarray(np.asarray(L0_attn_b_qkv, dtype=np.float32)),
        "gatew": np.ascontiguousarray(np.asarray(L0_ffn_gate_w, dtype=np.float32)),
        "egu": np.ascontiguousarray(
            np.asarray(L0_ffn_experts_gate_up, dtype=np.float32)),
        "edown": np.ascontiguousarray(
            np.asarray(L0_ffn_experts_down, dtype=np.float32)),
        "wg": np.ascontiguousarray(np.asarray(L0_ffn_shared_gate_w, dtype=np.float32)),
        "wu": np.ascontiguousarray(np.asarray(L0_ffn_shared_up_w, dtype=np.float32)),
        "wd": np.ascontiguousarray(np.asarray(L0_ffn_shared_down_w, dtype=np.float32)),
        "segw": np.ascontiguousarray(
            np.asarray(L0_ffn_shared_expert_gate_w, dtype=np.float32)),
    }
    if _NC_CACHE[0] is None:
        _NC_CACHE[0] = build()
    nc = _NC_CACHE[0]
    in_maps = []
    for c in range(NCORES):
        m = {"x": np.ascontiguousarray(x[B * c:B * c + B].reshape(T, D))}
        m.update(weights)
        in_maps.append(m)
    res = run_bass_kernel_spmd(nc, in_maps, core_ids=list(range(NCORES)))
    shards = [np.asarray(r["out"], dtype=np.float32).reshape(B, S, D)
              for r in res.results]
    return np.ascontiguousarray(np.concatenate(shards, axis=0))


# version 14 (final)
# speedup vs baseline: 1.1728x; 1.1728x over previous
import contextlib

import numpy as np

import concourse.bass as bass
import concourse.mybir as mybir
from concourse.bass_utils import run_bass_kernel_spmd

F32 = mybir.dt.float32
BF16 = mybir.dt.bfloat16
I32 = mybir.dt.int32
AF = mybir.ActivationFunctionType
ALU = mybir.AluOpType
AXX = mybir.AxisListType.X

NCORES = 8
B, S, D = 2, 2048, 1024
T = B * S
NH, HP = 16, 64
NTT = T // 128
ND = D // 128
FS = 2816
FE = 1408
NE = 8
ESH = 40.0
EPS = 1e-5
CAP = 1536
NSC = CAP // 512
NST = CAP // 128
BIGS = 1.0e5

ENGINES = ("sync", "gpsimd", "tensor", "scalar", "vector")


def dap(t, off, *dims):
    return bass.AP(t, off, [list(d) for d in dims])


class Phase:

    def __init__(self, nc, sems, cnt):
        self.nc = nc
        self.sems = sems
        self.cnt = cnt
        self.ops = {e: [] for e in ENGINES}

    def add(self, eng, fn, waits=(), inc=None, inc_n=1):
        w = [(s, v) for (s, v) in waits if v is not None and v > 0]
        self.ops[eng].append((w, fn, inc, inc_n))
        if inc is not None:
            self.cnt[inc] += inc_n
            return self.cnt[inc]
        return None

    def dma(self, eng, out, in_, chan, waits=(), slow=False):
        v = self.add(
            eng,
            lambda e, o=out, i=in_, s=slow: e.dma_start(
                out=o, in_=i, allow_slow_non_contiguous=s),
            waits=waits,
            inc=chan,
            inc_n=16,
        )
        return (chan, v)

    def st_waits(self):
        return [("CHS0", self.cnt["CHS0"]), ("CHS1", self.cnt["CHS1"])]

    def run(self):
        sems = self.sems
        with self.nc.Block() as blk:
            for eng_name in ENGINES:
                lst = self.ops[eng_name]
                if not lst:
                    continue

                def body(e, lst=lst):
                    last = {}
                    for (waits, fn, inc, inc_n) in lst:
                        for (s, v) in waits:
                            if last.get(s, 0) >= v:
                                continue
                            e.wait_ge(sems[s], v)
                            last[s] = v
                        ins = fn(e)
                        if inc is not None:
                            ins.then_inc(sems[inc], inc_n)

                getattr(blk, eng_name)(body)


def _ln_tile(p, src, st2, ht_out, waits_x, junk, consts):
    a_sq = p.add(
        "scalar",
        lambda e, x=src, s=st2: e.activation(
            out=junk[:, :], in_=x, func=AF.Square, accum_out=s[:, 0:1]),
        waits=waits_x,
        inc="ACT",
    )
    d_sum = p.add(
        "vector",
        lambda e, x=src, s=st2: e.reduce_sum(out=s[:, 1:2], in_=x, axis=AXX),
        waits=waits_x,
        inc="DVE",
    )
    a_mean = p.add(
        "scalar",
        lambda e, s=st2: e.mul(s[:, 2:3], s[:, 1:2], 1.0 / D),
        waits=[("DVE", d_sum)],
        inc="ACT",
    )
    a_m2 = p.add(
        "scalar",
        lambda e, s=st2: e.square(s[:, 3:4], s[:, 2:3]),
        waits=[("ACT", a_mean)],
        inc="ACT",
    )
    d_var = p.add(
        "vector",
        lambda e, s=st2: e.scalar_tensor_tensor(
            out=s[:, 4:5], in0=s[:, 0:1], scalar=1.0 / D, in1=s[:, 3:4],
            op0=ALU.mult, op1=ALU.subtract),
        waits=[("ACT", a_sq), ("ACT", a_m2)],
        inc="DVE",
    )
    a_sd = p.add(
        "scalar",
        lambda e, s=st2: e.activation(
            out=s[:, 5:6], in_=s[:, 4:5], func=AF.Sqrt, bias=consts[:, 0:1]),
        waits=[("DVE", d_var)],
        inc="ACT",
    )
    d_rec = p.add(
        "vector",
        lambda e, s=st2: e.reciprocal(out=s[:, 6:7], in_=s[:, 5:6]),
        waits=[("ACT", a_sd)],
        inc="DVE",
    )
    return d_rec


def _scatter_steps(p, experts, h2t, xwt, h2tmd, xar, xwar, posT_i, wTt_f,
                   d_pti, d_gi):
    hold = {}

    def _setup(e):
        bc = e.alloc_register()
        e.reg_mov(bc, CAP - 1)
        hold["bc"] = bc
        return e.nop()

    p.add("gpsimd", _setup)
    pair_lm = {}
    experts = list(experts)
    steps = []

    def mk_step(duo, tile):
        def step():
            for ee in duo:
                gs = ee % 2
                w = [("DVE", d_pti), ("DVE", d_gi)]
                if (ee, tile - 1) in pair_lm:
                    w.append(pair_lm[(ee, tile - 1)])
                elif (ee - 2, NTT - 1) in pair_lm:
                    w.append(pair_lm[(ee - 2, NTT - 1)])
                ldh = p.dma("sync", h2t[:, gs, :],
                            h2tmd.ap()[128 * tile:128 * tile + 128, :],
                            f"CHL{4 + gs}", waits=w)
                d_xw = p.add(
                    "vector",
                    lambda e, tile=tile, ee=ee, gs=gs: e.tensor_scalar(
                        out=xwt[:, gs, :], in0=h2t[:, gs, :],
                        scalar1=wTt_f[:, tile, ee:ee + 1], scalar2=None,
                        op0=ALU.mult),
                    waits=w + [ldh],
                    inc="DVE",
                )
                p.add(
                    "gpsimd",
                    lambda e, tile=tile, ee=ee, gs=gs: e.indirect_dma_start(
                        out=xar.ap(),
                        out_offset=bass.IndirectOffsetOnAxis(
                            posT_i[:, tile, ee:ee + 1], 0),
                        in_=h2t[:, gs, :],
                        in_offset=None,
                        element_offset=ee * CAP * D,
                        bounds_check=hold["bc"],
                        oob_is_err=False),
                    waits=[ldh],
                    inc=f"CHX{ee}",
                    inc_n=16,
                )
                sxw = p.add(
                    "gpsimd",
                    lambda e, tile=tile, ee=ee, gs=gs: e.indirect_dma_start(
                        out=xwar.ap(),
                        out_offset=bass.IndirectOffsetOnAxis(
                            posT_i[:, tile, ee:ee + 1], 0),
                        in_=xwt[:, gs, :],
                        in_offset=None,
                        element_offset=ee * CAP * D,
                        bounds_check=hold["bc"],
                        oob_is_err=False),
                    waits=[("DVE", d_xw)],
                    inc=f"CHX{ee}",
                    inc_n=16,
                )
                pair_lm[(ee, tile)] = (f"CHX{ee}", sxw)
        return step

    for e0 in range(0, len(experts), 2):
        duo = experts[e0:e0 + 2]
        for tile in range(NTT):
            steps.append(mk_step(duo, tile))
    return steps


def build():
    nc = bass.Bass("TRN2", target_bir_lowering=False)

    x_in = nc.dram_tensor("x", [T, D], F32, kind="ExternalInput").ap()
    wqkv = nc.dram_tensor("wqkv", [D, 3 * D], F32, kind="ExternalInput")
    bqkv = nc.dram_tensor("bqkv", [3 * D], F32, kind="ExternalInput")
    gatew = nc.dram_tensor("gatew", [NE, D], F32, kind="ExternalInput").ap()
    egu = nc.dram_tensor("egu", [NE, 2 * FE, D], F32, kind="ExternalInput").ap()
    edown = nc.dram_tensor("edown", [NE, D, FE], F32, kind="ExternalInput").ap()
    wg = nc.dram_tensor("wg", [FS, D], F32, kind="ExternalInput").ap()
    wu = nc.dram_tensor("wu", [FS, D], F32, kind="ExternalInput").ap()
    wd = nc.dram_tensor("wd", [D, FS], F32, kind="ExternalInput").ap()
    segw = nc.dram_tensor("segw", [1, D], F32, kind="ExternalInput").ap()
    out_t = nc.dram_tensor("out", [T, D], F32, kind="ExternalOutput").ap()

    qdh = nc.dram_tensor("qdh", [D, T], BF16)
    qdl = nc.dram_tensor("qdl", [D, T], BF16)
    kdh = nc.dram_tensor("kdh", [D, T], BF16)
    kdl = nc.dram_tensor("kdl", [D, T], BF16)
    vd = nc.dram_tensor("vd", [T, D], F32)
    attnd = nc.dram_tensor("attnd", [T, D], F32)
    x1d = nc.dram_tensor("x1d", [T, D], F32)
    wTd = nc.dram_tensor("wTd", [9, T], F32)
    hidd_sh = nc.dram_tensor("hidd_sh", [FS, T], BF16)
    hidd_sl = nc.dram_tensor("hidd_sl", [FS, T], BF16)
    h2tmd = nc.dram_tensor("h2tmd", [T, D], F32)
    xar = nc.dram_tensor("xar", [NE * CAP, D], F32)
    xwar = nc.dram_tensor("xwar", [NE * CAP, D], F32)
    dar = nc.dram_tensor("dar", [NE * CAP, D], F32)
    shoutd = nc.dram_tensor("shoutd", [T, D], F32)

    cnt = {"CHC": 0, "CHL0": 0, "CHL1": 0, "CHL2": 0, "CHL3": 0,
           "CHL4": 0, "CHL5": 0,
           "CHS0": 0, "CHS1": 0, "PE": 0, "ACT": 0, "DVE": 0, "GPS": 0}
    for _e in range(NE):
        cnt[f"CHX{_e}"] = 0

    with contextlib.ExitStack() as top:
        sems = {n: top.enter_context(nc.semaphore(n)) for n in cnt}
        ident = top.enter_context(nc.sbuf_tensor("ident", [128, 128], F32))
        identb = top.enter_context(nc.sbuf_tensor("identb", [128, 128], BF16))
        consts = top.enter_context(nc.sbuf_tensor("consts", [128, 2], F32))
        eiota = top.enter_context(nc.sbuf_tensor("eiota", [128, 8], F32))
        zsb = top.enter_context(nc.sbuf_tensor("zsb", [128, 1024], F32))
        posT_i = top.enter_context(nc.sbuf_tensor("posT_i", [128, NTT, 8], I32))
        wTt_f = top.enter_context(nc.sbuf_tensor("wTt_f", [128, NTT, 8], F32))
        gidx_i = top.enter_context(nc.sbuf_tensor("gidx_i", [128, NTT, 2], I32))

        p = Phase(nc, sems, cnt)
        with nc.sbuf_tensor("iob", [128, 128], F32) as iob:
            p.add("gpsimd", lambda e: e.memset(consts[:, 0:1], EPS), inc="GPS")
            p.add("gpsimd", lambda e: e.memset(consts[:, 1:2], -ESH), inc="GPS")
            p.add("gpsimd", lambda e: e.iota(
                eiota[:, :], [[1, 8]], channel_multiplier=0,
                allow_small_or_imprecise_dtypes=True), inc="GPS")
            g_zs = p.add("gpsimd", lambda e: e.memset(zsb[:, :], 0.0), inc="GPS")
            g_iota = p.add(
                "gpsimd",
                lambda e: e.iota(iob[:, :], [[1, 128]], channel_multiplier=-1,
                                 allow_small_or_imprecise_dtypes=True),
                inc="GPS",
            )
            for ar in (xar, xwar):
                for r0 in range(0, NE * CAP, 1024):
                    p.dma(
                        "sync",
                        dap(ar, r0 * D, (D, 128), (128 * D, 8), (1, D)),
                        bass.AP(zsb, 0, [[1024, 128], [0, 8], [1, 1024]]),
                        "CHL2" if ar is xar else "CHL3",
                        waits=[("GPS", g_zs)])
            zf2, zf3 = ("CHL2", cnt["CHL2"]), ("CHL3", cnt["CHL3"])
            dvi = p.add(
                "vector",
                lambda e: e.tensor_scalar(out=ident[:, :], in0=iob[:, :],
                                          scalar1=0.0, scalar2=None,
                                          op0=ALU.is_equal),
                waits=[("GPS", g_iota)],
                inc="DVE",
            )
            p.add(
                "vector",
                lambda e: e.tensor_copy(out=identb[:, :], in_=ident[:, :]),
                waits=[("DVE", dvi)],
                inc="DVE",
            )
            p.run()

        with contextlib.ExitStack() as st1:
            h1 = st1.enter_context(nc.sbuf_tensor("h1", [128, ND, T], F32))
            b_sb = st1.enter_context(nc.sbuf_tensor("b_sb", [128, 16], F32))
            bvb = st1.enter_context(nc.sbuf_tensor("bvb", [128, D], F32))

            p = Phase(nc, sems, cnt)
            with contextlib.ExitStack() as ps:
                xt = ps.enter_context(nc.sbuf_tensor("xt", [128, 2, D], F32))
                junk = ps.enter_context(nc.sbuf_tensor("junk", [128, D], F32))
                stats = ps.enter_context(nc.sbuf_tensor("stats", [128, 2, 8], F32))
                ht = ps.enter_context(nc.sbuf_tensor("ht", [128, 2, D], F32))
                pbank = [ps.enter_context(nc.psum_tensor(f"p1{i}", [128, 512], F32))
                         for i in range(2)]
                bsb_ld = p.dma("sync", b_sb[:, :],
                               dap(bqkv, 0, (1, 128), (128, 16)), "CHC",
                               slow=True)
                bvb_ld = p.dma("sync", bvb[:, :],
                               dap(bqkv, 2 * D, (0, 128), (1, D)), "CHC")
                ld = {}
                ht_done = {}
                tr_done = {}
                ev_done = {}
                for tt in range(NTT):
                    w = [("DVE", ht_done[tt - 2])] if tt >= 2 else []
                    ld[tt] = p.dma("sync", xt[:, tt % 2, :],
                                   x_in[128 * tt:128 * tt + 128, :],
                                   f"CHL{tt % 2}", waits=w)
                    st2 = stats[:, tt % 2, :]
                    src = xt[:, tt % 2, :]
                    d_rec = _ln_tile(p, src, st2, None, [ld[tt]], junk,
                                     consts)
                    w = [("PE", tr_done[tt - 2])] if tt >= 2 else []
                    w.append(("DVE", d_rec))
                    ht_done[tt] = p.add(
                        "vector",
                        lambda e, x=src, s=st2, h=ht[:, tt % 2, :]: e.tensor_scalar(
                            out=h, in0=x, scalar1=s[:, 2:3], scalar2=s[:, 6:7],
                            op0=ALU.subtract, op1=ALU.mult),
                        waits=w,
                        inc="DVE",
                    )
                    for half in range(2):
                        pb = pbank[half]
                        pe_t = None
                        for dd in range(4):
                            d = 4 * half + dd
                            w = [("DVE", ht_done[tt])]
                            if tt >= 1:
                                w.append(("DVE", ev_done[(tt - 1, half)]))
                            pe_t = p.add(
                                "tensor",
                                lambda e, d=d, dd=dd, pb=pb, h=ht[:, tt % 2, :]:
                                    e.transpose(pb[:, 128 * dd:128 * dd + 128],
                                                h[:, 128 * d:128 * d + 128],
                                                ident[:, :]),
                                waits=w,
                                inc="PE",
                            )
                        tr_done[tt] = pe_t
                        ev_done[(tt, half)] = p.add(
                            "vector",
                            lambda e, pb=pb, tt=tt, half=half: e.tensor_copy(
                                out=h1[:, 4 * half:4 * half + 4,
                                       128 * tt:128 * tt + 128],
                                in_=pb[:, :].rearrange("p (a b) -> p a b", a=4)),
                            waits=[("PE", pe_t)],
                            inc="DVE",
                        )
                p.run()

            p = Phase(nc, sems, cnt)
            with contextlib.ExitStack() as ps:
                wbuf = ps.enter_context(nc.sbuf_tensor("wbuf", [128, 2, ND, 128], F32))
                qkstage = ps.enter_context(nc.sbuf_tensor("qkstage", [128, 2, 512], F32))
                qksh = ps.enter_context(nc.sbuf_tensor("qksh", [128, 2, 512], BF16))
                qksl = ps.enter_context(nc.sbuf_tensor("qksl", [128, 2, 512], BF16))
                pbank = [ps.enter_context(nc.psum_tensor(f"p2{i}", [128, 512], F32))
                         for i in range(2)]
                ldm = {}
                pe_end = {}
                act_ev = {}
                psum_ev = {}
                st_done = {}
                for m in range(16):
                    w = [("PE", pe_end[(m - 2, 7)])] if m >= 2 else []
                    ldm[m] = p.dma(
                        "sync", wbuf[:, m % 2, :, :],
                        dap(wqkv, 128 * m, (3 * D, 128), (128 * 3 * D, ND), (1, 128)),
                        f"CHL{m % 2}", waits=w)
                    for n in range(8):
                        idx = 8 * m + n
                        pb = pbank[idx % 2]
                        for d in range(ND):
                            w = [ldm[m]]
                            if idx >= 2 and d == 0:
                                w.append(("ACT", psum_ev[idx - 2]))
                            pe_end[(m, n)] = p.add(
                                "tensor",
                                lambda e, pb=pb, m=m, n=n, d=d: e.matmul(
                                    pb[:, :], wbuf[:, m % 2, d, :],
                                    h1[:, d, 512 * n:512 * n + 512],
                                    start=(d == 0), stop=(d == ND - 1)),
                                waits=w,
                                inc="PE" if d == ND - 1 else None,
                            )
                        w = [("PE", pe_end[(m, n)]), bvb_ld]
                        if idx >= 2:
                            w.append(("DVE", act_ev[idx - 2]))
                        psum_ev[idx] = p.add(
                            "scalar",
                            lambda e, pb=pb, m=m, idx=idx: e.activation(
                                out=qkstage[:, idx % 2, :], in_=pb[:, :],
                                func=AF.Identity, bias=b_sb[:, m:m + 1]),
                            waits=w,
                            inc="ACT",
                        )
                        w = [("ACT", psum_ev[idx])]
                        if idx >= 2:
                            w.append(st_done[idx - 2])
                        a_hi = p.add(
                            "scalar",
                            lambda e, idx=idx: e.copy(
                                out=qksh[:, idx % 2, :],
                                in_=qkstage[:, idx % 2, :]),
                            waits=w,
                            inc="ACT",
                        )
                        act_ev[idx] = p.add(
                            "vector",
                            lambda e, idx=idx: e.tensor_tensor(
                                out=qksl[:, idx % 2, :],
                                in0=qkstage[:, idx % 2, :],
                                in1=qksh[:, idx % 2, :], op=ALU.subtract),
                            waits=[("ACT", a_hi)],
                            inc="DVE",
                        )
                        dsth = qdh if m < 8 else kdh
                        dstl = qdl if m < 8 else kdl
                        p.dma(
                            "gpsimd",
                            dap(dsth, (128 * (m % 8)) * T + 512 * n, (T, 128),
                                (1, 512)),
                            qksh[:, idx % 2, :], f"CHS{idx % 2}",
                            waits=[("DVE", act_ev[idx])])
                        st_done[idx] = p.dma(
                            "gpsimd",
                            dap(dstl, (128 * (m % 8)) * T + 512 * n, (T, 128),
                                (1, 512)),
                            qksl[:, idx % 2, :], f"CHS{idx % 2}",
                            waits=[("DVE", act_ev[idx])])
                p.add("gpsimd", lambda e: e.nop(), waits=p.st_waits())
                p.run()

            p = Phase(nc, sems, cnt)
            with contextlib.ExitStack() as ps:
                wv = ps.enter_context(nc.sbuf_tensor("wv", [128, ND, D], F32))
                vstage = ps.enter_context(nc.sbuf_tensor("vstage", [128, 2, D], F32))
                pv = [ps.enter_context(nc.psum_tensor(f"p2v{i}", [128, 512], F32))
                      for i in range(4)]
                ldv = p.dma(
                    "sync", wv[:, :, :],
                    dap(wqkv, 2 * D, (3 * D, 128), (128 * 3 * D, ND), (1, D)),
                    "CHC")
                pe_end = {}
                dve_ev = {}
                st_done = {}
                for tt in range(NTT):
                    for c in range(2):
                        pb = pv[2 * (tt % 2) + c]
                        for d in range(ND):
                            w = [ldv]
                            if tt >= 2 and d == 0:
                                w.append(("DVE", dve_ev[(tt - 2, c)]))
                            pe_end[(tt, c)] = p.add(
                                "tensor",
                                lambda e, pb=pb, tt=tt, c=c, d=d: e.matmul(
                                    pb[:, :], h1[:, d, 128 * tt:128 * tt + 128],
                                    wv[:, d, 512 * c:512 * c + 512],
                                    start=(d == 0), stop=(d == ND - 1)),
                                waits=w,
                                inc="PE" if d == ND - 1 else None,
                            )
                        w = [("PE", pe_end[(tt, c)]), bvb_ld]
                        if tt >= 2:
                            w.append(st_done[tt - 2])
                        dve_ev[(tt, c)] = p.add(
                            "vector",
                            lambda e, pb=pb, tt=tt, c=c: e.tensor_tensor(
                                out=vstage[:, tt % 2, 512 * c:512 * c + 512],
                                in0=pb[:, :], in1=bvb[:, 512 * c:512 * c + 512],
                                op=ALU.add),
                            waits=w,
                            inc="DVE",
                        )
                    st_done[tt] = p.dma(
                        "gpsimd", vd.ap()[128 * tt:128 * tt + 128, :],
                        vstage[:, tt % 2, :], f"CHS{tt % 2}",
                        waits=[("DVE", dve_ev[(tt, 1)])])
                p.add("gpsimd", lambda e: e.nop(), waits=p.st_waits())
                p.run()

        with contextlib.ExitStack() as st2s:
            q_hh = st2s.enter_context(nc.sbuf_tensor("q_hh", [128, S], BF16))
            q_ll = st2s.enter_context(nc.sbuf_tensor("q_ll", [128, S], BF16))
            k_st = st2s.enter_context(nc.sbuf_tensor("k_st", [128, S], BF16))
            v65 = st2s.enter_context(nc.sbuf_tensor("v65", [128, 16, 65], F32))
            p_sb = st2s.enter_context(nc.sbuf_tensor("p_sb", [128, 16, 512], F32))
            o_raw = st2s.enter_context(nc.sbuf_tensor("o_raw", [128, 2, 512], F32))
            a_stage = st2s.enter_context(
                nc.sbuf_tensor("a_stage", [128, 2, 4, 64], F32))
            rs = st2s.enter_context(nc.sbuf_tensor("rs", [128, 2, 4], F32))
            pS = [st2s.enter_context(nc.psum_tensor(f"pS{i}", [128, 512], F32))
                  for i in range(4)]
            pO = [st2s.enter_context(nc.psum_tensor(f"pO{i}", [128, 512], F32))
                  for i in range(2)]
            pT = [st2s.enter_context(nc.psum_tensor(f"pT{i}", [128, 512], F32))
                  for i in range(2)]
            for s in range(B):
                for h in range(NH):
                    p = Phase(nc, sems, cnt)
                    p.dma("sync", q_hh[0:64, :],
                          dap(qdh, (64 * h) * T + S * s, (T, 64), (1, S)),
                          "CHL0")
                    p.dma("sync", q_hh[64:128, :],
                          dap(qdh, (64 * h) * T + S * s, (T, 64), (1, S)),
                          "CHL0")
                    p.dma("sync", q_ll[0:64, :],
                          dap(qdl, (64 * h) * T + S * s, (T, 64), (1, S)),
                          "CHL0")
                    ldq = p.dma("sync", q_ll[64:128, :],
                                dap(qdl, (64 * h) * T + S * s, (T, 64), (1, S)),
                                "CHL0")
                    p.dma("sync", k_st[0:64, :],
                          dap(kdh, (64 * h) * T + S * s, (T, 64), (1, S)),
                          "CHL1")
                    ldk = p.dma("sync", k_st[64:128, :],
                                dap(kdl, (64 * h) * T + S * s, (T, 64), (1, S)),
                                "CHL1")
                    ldvl = None
                    for j in range(16):
                        ldvl = p.dma(
                            "sync", v65[:, j, 0:64],
                            dap(vd, (S * s + 128 * j) * D + 64 * h, (D, 128),
                                (1, 64)),
                            "CHL2")
                    g_ones = p.add(
                        "gpsimd",
                        lambda e: e.memset(v65[:, :, 64:65], 1.0),
                        inc="GPS",
                    )
                    exp_a = {}
                    smm = {}
                    pv_end = {}
                    oev = {}
                    tr_end = {}
                    scl = {}
                    stn = {}
                    for ic in range(4):
                        for j in range(16):
                            idx = 16 * ic + j
                            w = [ldk, ldq]
                            if idx >= 4:
                                w.append(("ACT", exp_a[idx - 4]))
                            p.add(
                                "tensor",
                                lambda e, ic=ic, j=j, pb=pS[idx % 4]: e.matmul(
                                    pb[:, :], k_st[:, 128 * j:128 * j + 128],
                                    q_hh[:, 512 * ic:512 * ic + 512],
                                    start=True, stop=False),
                                waits=w,
                            )
                            smm[idx] = p.add(
                                "tensor",
                                lambda e, ic=ic, j=j, pb=pS[idx % 4]: e.matmul(
                                    pb[:, :], k_st[:, 128 * j:128 * j + 128],
                                    q_ll[:, 512 * ic:512 * ic + 512],
                                    start=False, stop=True),
                                inc="PE",
                            )
                            w = [("PE", smm[idx])]
                            if ic >= 1 and j == 0:
                                w.append(("PE", pv_end[ic - 1]))
                            exp_a[idx] = p.add(
                                "scalar",
                                lambda e, j=j, pb=pS[idx % 4]: e.activation(
                                    out=p_sb[:, j, :], in_=pb[:, :], func=AF.Exp,
                                    bias=consts[:, 1:2]),
                                waits=w,
                                inc="ACT",
                            )
                        for j in range(16):
                            idx = 16 * ic + j
                            w = [("ACT", exp_a[idx]), ldvl,
                                 ("GPS", g_ones)]
                            if ic >= 2 and j == 0:
                                w.append(("DVE", oev[ic - 2]))
                            pv_end[ic] = p.add(
                                "tensor",
                                lambda e, ic=ic, j=j, pb=pO[ic % 2]: e.matmul(
                                    pb[0:65, :], v65[:, j, :], p_sb[:, j, :],
                                    start=(j == 0), stop=(j == 15)),
                                waits=w,
                                inc="PE" if j == 15 else None,
                            )
                        oev[ic] = p.add(
                            "vector",
                            lambda e, ic=ic: e.tensor_copy(
                                out=o_raw[0:65, ic % 2, :],
                                in_=pO[ic % 2][0:65, :]),
                            waits=[("PE", pv_end[ic])],
                            inc="DVE",
                        )
                        for bb in range(4):
                            w = [("DVE", oev[ic])]
                            if ic >= 2 and bb == 0:
                                w.append(("DVE", scl[ic - 2]))
                            tr_end[ic] = p.add(
                                "tensor",
                                lambda e, ic=ic, bb=bb: e.transpose(
                                    pT[ic % 2][:, 65 * bb:65 * bb + 65],
                                    o_raw[0:65, ic % 2, 128 * bb:128 * bb + 128],
                                    ident[0:65, 0:65]),
                                waits=w,
                                inc="PE" if bb == 3 else None,
                            )
                        d_rec = p.add(
                            "vector",
                            lambda e, ic=ic, pb=pT[ic % 2]: e.reciprocal(
                                out=rs[:, ic % 2, :],
                                in_=bass.AP(pb, 64, [[512, 128], [65, 4]])),
                            waits=[("PE", tr_end[ic])],
                            inc="DVE",
                        )
                        for bb in range(4):
                            w = [("DVE", d_rec)]
                            if ic >= 2 and bb == 0:
                                w.append(stn[ic - 2])
                            scl[ic] = p.add(
                                "vector",
                                lambda e, ic=ic, bb=bb, pb=pT[ic % 2]:
                                    e.tensor_scalar(
                                        out=a_stage[:, ic % 2, bb, :],
                                        in0=pb[:, 65 * bb:65 * bb + 64],
                                        scalar1=rs[:, ic % 2, bb:bb + 1],
                                        scalar2=None, op0=ALU.mult),
                                waits=w,
                                inc="DVE",
                            )
                        stn[ic] = p.dma(
                            "gpsimd",
                            dap(attnd, (S * s + 128 * h) * D + 512 * ic * 64,
                                (64, 128), (128 * 64, 4), (1, 64)),
                            a_stage[:, ic % 2, :, :], f"CHS{ic % 2}",
                            waits=[("DVE", scl[ic])])
                    p.add("gpsimd", lambda e: e.nop(), waits=p.st_waits())
                    p.run()

        with contextlib.ExitStack() as st3:
            h2h = st3.enter_context(nc.sbuf_tensor("h2h", [128, ND, T], BF16))
            h2l = st3.enter_context(nc.sbuf_tensor("h2l", [128, ND, T], BF16))

            p = Phase(nc, sems, cnt)
            with contextlib.ExitStack() as ps:
                xt = ps.enter_context(nc.sbuf_tensor("xt4", [128, 2, D], F32))
                at = ps.enter_context(nc.sbuf_tensor("at4", [128, 2, D], F32))
                x1t = ps.enter_context(nc.sbuf_tensor("x1t", [128, 2, D], F32))
                junk = ps.enter_context(nc.sbuf_tensor("junk4", [128, D], F32))
                stats = ps.enter_context(nc.sbuf_tensor("stats4", [128, 2, 8], F32))
                ht = ps.enter_context(nc.sbuf_tensor("ht4", [128, 2, D], F32))
                hth = ps.enter_context(nc.sbuf_tensor("hth4", [128, 2, D], BF16))
                htl = ps.enter_context(nc.sbuf_tensor("htl4", [128, 2, D], BF16))
                pbank = [ps.enter_context(
                    nc.psum_tensor(f"p4{i}", [128, 512], BF16))
                    for i in range(2)]
                x1_done = {}
                ht_done = {}
                tr_done = {}
                ev_done = {}
                st_done = {}
                st2_h2 = {}
                for tt in range(NTT):
                    w = [("DVE", x1_done[tt - 2])] if tt >= 2 else []
                    ldx = p.dma("sync", xt[:, tt % 2, :],
                                x_in[128 * tt:128 * tt + 128, :],
                                f"CHL{tt % 2}", waits=w)
                    lda = p.dma("sync", at[:, tt % 2, :],
                                attnd.ap()[128 * tt:128 * tt + 128, :],
                                f"CHL{2 + tt % 2}", waits=w)
                    w = [ldx, lda]
                    if tt >= 2:
                        w.append(st2_h2[tt - 2])
                    x1_done[tt] = p.add(
                        "vector",
                        lambda e, tt=tt: e.tensor_tensor(
                            out=x1t[:, tt % 2, :], in0=xt[:, tt % 2, :],
                            in1=at[:, tt % 2, :], op=ALU.add),
                        waits=w,
                        inc="DVE",
                    )
                    st_done[tt] = p.dma(
                        "gpsimd", x1d.ap()[128 * tt:128 * tt + 128, :],
                        x1t[:, tt % 2, :], f"CHS{tt % 2}",
                        waits=[("DVE", x1_done[tt])])
                    st2 = stats[:, tt % 2, :]
                    src = x1t[:, tt % 2, :]
                    d_rec = _ln_tile(p, src, st2, None, [("DVE", x1_done[tt])],
                                     junk, consts)
                    w = [("PE", tr_done[tt - 2])] if tt >= 2 else []
                    w.append(("DVE", d_rec))
                    if tt >= 2:
                        w.append(st2_h2[tt - 2])
                    ht_done[tt] = p.add(
                        "vector",
                        lambda e, x=src, s=st2, hh=ht[:, tt % 2, :]: e.tensor_scalar(
                            out=hh, in0=x, scalar1=s[:, 2:3], scalar2=s[:, 6:7],
                            op0=ALU.subtract, op1=ALU.mult),
                        waits=w,
                        inc="DVE",
                    )
                    st2_h2[tt] = p.dma(
                        "gpsimd", h2tmd.ap()[128 * tt:128 * tt + 128, :],
                        ht[:, tt % 2, :], f"CHS{tt % 2}",
                        waits=[("DVE", ht_done[tt])])
                    s_hi = p.add(
                        "scalar",
                        lambda e, tt=tt: e.copy(out=hth[:, tt % 2, :],
                                                in_=ht[:, tt % 2, :]),
                        waits=[("DVE", ht_done[tt])] + (
                            [("PE", tr_done[tt - 2])] if tt >= 2 else []),
                        inc="ACT",
                    )
                    d_lo = p.add(
                        "vector",
                        lambda e, tt=tt: e.tensor_tensor(
                            out=htl[:, tt % 2, :], in0=ht[:, tt % 2, :],
                            in1=hth[:, tt % 2, :], op=ALU.subtract),
                        waits=[("ACT", s_hi)],
                        inc="DVE",
                    )
                    for half in range(2):
                        pb = pbank[half]
                        pe_t = None
                        prev_ev = None
                        for (srcb, dstb) in ((hth, h2h), (htl, h2l)):
                            for dd in range(4):
                                d = 4 * half + dd
                                w = [("DVE", d_lo)]
                                if prev_ev is not None:
                                    w.append(("DVE", prev_ev))
                                elif tt >= 1:
                                    w.append(("DVE", ev_done[(tt - 1, half)]))
                                pe_t = p.add(
                                    "tensor",
                                    lambda e, d=d, dd=dd, pb=pb,
                                           hh=srcb[:, tt % 2, :]:
                                        e.transpose(
                                            pb[:, 128 * dd:128 * dd + 128],
                                            hh[:, 128 * d:128 * d + 128],
                                            identb[:, :]),
                                    waits=w,
                                    inc="PE",
                                )
                            ev_done[(tt, half)] = p.add(
                                "vector",
                                lambda e, pb=pb, tt=tt, half=half, dstb=dstb:
                                    e.tensor_copy(
                                        out=dstb[:, 4 * half:4 * half + 4,
                                                 128 * tt:128 * tt + 128],
                                        in_=pb[:, :].rearrange(
                                            "p (a b) -> p a b", a=4)),
                                waits=[("PE", pe_t)],
                                inc="DVE",
                            )
                            prev_ev = ev_done[(tt, half)]
                        tr_done[tt] = pe_t
                p.add("gpsimd", lambda e: e.nop(), waits=p.st_waits())
                p.run()

            p = Phase(nc, sems, cnt)
            with contextlib.ExitStack() as ps:
                gw_s = ps.enter_context(nc.sbuf_tensor("gw_s", [9, D], F32))
                gwTh = ps.enter_context(nc.sbuf_tensor("gwTh", [128, ND * 16], BF16))
                gwTl = ps.enter_context(nc.sbuf_tensor("gwTl", [128, ND * 16], BF16))
                wT_sb = ps.enter_context(nc.sbuf_tensor("wT_sb", [9, T], F32))
                lg = ps.enter_context(nc.sbuf_tensor("lg", [16, 2, 512], F32))
                lt = ps.enter_context(nc.sbuf_tensor("lt", [128, 2, 4, 16], F32))
                rbuf = ps.enter_context(nc.sbuf_tensor("rbuf", [128, 40], F32))
                eT = ps.enter_context(nc.sbuf_tensor("eT", [128, NTT, 2], F32))
                indT = ps.enter_context(nc.sbuf_tensor("indT", [8, T], F32))
                posb = ps.enter_context(nc.sbuf_tensor("posb", [8, T], F32))
                posT_f = ps.enter_context(
                    nc.sbuf_tensor("posT_f", [128, NTT, 8], F32))
                gidx_f = ps.enter_context(
                    nc.sbuf_tensor("gidx_f", [128, NTT, 2], F32))
                w16 = ps.enter_context(nc.sbuf_tensor("w16", [128, 2, 4, 16], F32))
                pL = [ps.enter_context(nc.psum_tensor(f"p5l{i}", [128, 512], F32))
                      for i in range(2)]
                pT2 = [ps.enter_context(nc.psum_tensor(f"p5t{i}", [128, 512], F32))
                       for i in range(2)]
                pW = [ps.enter_context(nc.psum_tensor(f"p5w{i}", [128, 512], F32))
                      for i in range(2)]
                ldg = p.dma("sync", gw_s[0:8, :], gatew, "CHC")
                ldg = p.dma("sync", gw_s[8:9, :], segw, "CHC")
                gwt_ev = {}
                for d in range(ND):
                    w = [ldg]
                    if d >= 2:
                        w.append(("DVE", gwt_ev[d - 2]))
                    pe_t = p.add(
                        "tensor",
                        lambda e, d=d, pb=pW[d % 2]: e.transpose(
                            pb[:, 0:9], gw_s[0:9, 128 * d:128 * d + 128],
                            ident[0:9, 0:9]),
                        waits=w,
                        inc="PE",
                    )
                    g_hi = p.add(
                        "scalar",
                        lambda e, d=d, pb=pW[d % 2]: e.copy(
                            out=gwTh[:, 16 * d:16 * d + 9], in_=pb[:, 0:9]),
                        waits=[("PE", pe_t)],
                        inc="ACT",
                    )
                    gwt_ev[d] = p.add(
                        "vector",
                        lambda e, d=d, pb=pW[d % 2]: e.scalar_tensor_tensor(
                            out=gwTl[:, 16 * d:16 * d + 9], in0=pb[:, 0:9],
                            scalar=1.0, in1=gwTh[:, 16 * d:16 * d + 9],
                            op0=ALU.mult, op1=ALU.subtract),
                        waits=[("ACT", g_hi)],
                        inc="DVE",
                    )
                gwt_last = gwt_ev[ND - 1]
                lg_ev = {}
                lt_ev = {}
                w_done = {}
                wt_ev = {}
                for tc in range(8):
                    mm_end = None
                    np3 = 0
                    for (ga, hb) in ((0, 0), (0, 1), (1, 0)):
                        gsrc = (gwTh if ga == 0 else gwTl)
                        hsrc = (h2h if hb == 0 else h2l)
                        for d in range(ND):
                            w = [("DVE", gwt_last)]
                            if tc >= 2 and np3 == 0:
                                w.append(("DVE", lg_ev[tc - 2]))
                            mm_end = p.add(
                                "tensor",
                                lambda e, tc=tc, d=d, gs=gsrc, hs=hsrc,
                                       st=(np3 == 0), sp=(np3 == 23),
                                       pb=pL[tc % 2]: e.matmul(
                                    pb[0:9, :], gs[:, 16 * d:16 * d + 9],
                                    hs[:, d, 512 * tc:512 * tc + 512],
                                    start=st, stop=sp),
                                waits=w,
                                inc="PE" if np3 == 23 else None,
                            )
                            np3 += 1
                    lg_ev[tc] = p.add(
                        "vector",
                        lambda e, tc=tc, pb=pL[tc % 2]: e.tensor_copy(
                            out=lg[0:9, tc % 2, :], in_=pb[0:9, :]),
                        waits=[("PE", mm_end)],
                        inc="DVE",
                    )
                    t_end = None
                    for bb in range(4):
                        w = [("DVE", lg_ev[tc])]
                        if tc >= 2 and bb == 0:
                            w.append(("DVE", lt_ev[tc - 2]))
                        t_end = p.add(
                            "tensor",
                            lambda e, tc=tc, bb=bb, pb=pT2[tc % 2]: e.transpose(
                                pb[:, 9 * bb:9 * bb + 9],
                                lg[0:9, tc % 2, 128 * bb:128 * bb + 128],
                                ident[0:9, 0:9]),
                            waits=w,
                            inc="PE" if bb == 3 else None,
                        )
                    w = [("PE", t_end)]
                    if tc >= 2:
                        w.append(("ACT", w_done[tc - 2]))
                    lt_ev[tc] = p.add(
                        "vector",
                        lambda e, tc=tc, pb=pT2[tc % 2]: e.tensor_copy(
                            out=lt[:, tc % 2, :, 0:9],
                            in_=pb[:, 0:36].rearrange("p (b n) -> p b n", b=4)),
                        waits=w,
                        inc="DVE",
                    )
                    d_w = None
                    for bb in range(4):
                        ltb = lt[:, tc % 2, bb, :]
                        d1 = p.add(
                            "vector",
                            lambda e, ltb=ltb: e.reduce_max(
                                out=rbuf[:, 8:9], in_=ltb[:, 0:8], axis=AXX,
                                negate=True),
                            waits=[("DVE", lt_ev[tc])],
                            inc="DVE",
                        )
                        a1 = p.add(
                            "scalar",
                            lambda e, ltb=ltb: e.activation(
                                out=rbuf[:, 0:8], in_=ltb[:, 0:8], func=AF.Exp,
                                bias=rbuf[:, 8:9], accum_out=rbuf[:, 9:10]),
                            waits=[("DVE", d1)],
                            inc="ACT",
                        )
                        d2 = p.add(
                            "vector",
                            lambda e: e.reciprocal(out=rbuf[:, 10:11],
                                                   in_=rbuf[:, 9:10]),
                            waits=[("ACT", a1)],
                            inc="DVE",
                        )
                        dsc = p.add(
                            "vector",
                            lambda e: e.tensor_scalar(
                                out=rbuf[:, 0:8], in0=rbuf[:, 0:8],
                                scalar1=rbuf[:, 10:11], scalar2=None,
                                op0=ALU.mult),
                            waits=[("DVE", d2)],
                            inc="DVE",
                        )
                        prevd = dsc
                        for ee in range(8):
                            prevd = p.add(
                                "vector",
                                lambda e, ee=ee: e.tensor_scalar(
                                    out=rbuf[:, 16:24], in0=rbuf[:, 0:8],
                                    scalar1=rbuf[:, ee:ee + 1], scalar2=None,
                                    op0=ALU.is_gt, op1=ALU.add,
                                    accum_out=rbuf[:, 12:13]),
                                waits=[("DVE", prevd)],
                                inc="DVE",
                            )
                            if ee > 0:
                                prevd = p.add(
                                    "vector",
                                    lambda e, ee=ee: e.tensor_scalar(
                                        out=rbuf[:, 16:16 + ee], in0=rbuf[:, 0:ee],
                                        scalar1=rbuf[:, ee:ee + 1], scalar2=None,
                                        op0=ALU.is_equal, op1=ALU.add,
                                        accum_out=rbuf[:, 13:14]),
                                    waits=[("DVE", prevd)],
                                    inc="DVE",
                                )
                                prevd = p.add(
                                    "vector",
                                    lambda e: e.tensor_tensor(
                                        out=rbuf[:, 12:13], in0=rbuf[:, 12:13],
                                        in1=rbuf[:, 13:14], op=ALU.add),
                                    waits=[("DVE", prevd)],
                                    inc="DVE",
                                )
                            prevd = p.add(
                                "vector",
                                lambda e: e.tensor_scalar(
                                    out=rbuf[:, 14:15], in0=rbuf[:, 12:13],
                                    scalar1=2.0, scalar2=None, op0=ALU.is_lt),
                                waits=[("DVE", prevd)],
                                inc="DVE",
                            )
                            d_w = p.add(
                                "vector",
                                lambda e, ee=ee, tc=tc, bb=bb: e.tensor_tensor(
                                    out=w16[:, tc % 2, bb, ee:ee + 1],
                                    in0=rbuf[:, ee:ee + 1],
                                    in1=rbuf[:, 14:15], op=ALU.mult),
                                waits=[("DVE", prevd)],
                                inc="DVE",
                            )
                            prevd = d_w
                        w_done[tc] = p.add(
                            "scalar",
                            lambda e, tc=tc, bb=bb, ltb=ltb: e.activation(
                                out=w16[:, tc % 2, bb, 8:9], in_=ltb[:, 8:9],
                                func=AF.Sigmoid),
                            waits=[("DVE", d_w)],
                            inc="ACT",
                        )
                        tile = 4 * tc + bb
                        wv = w16[:, tc % 2, bb, 0:8]
                        pv = p.add("vector", lambda e, wv=wv: e.reduce_max(
                            out=rbuf[:, 16:17], in_=wv, axis=AXX),
                            waits=[("DVE", d_w)], inc="DVE")
                        pv = p.add("vector", lambda e, wv=wv: e.tensor_scalar(
                            out=rbuf[:, 24:32], in0=wv, scalar1=rbuf[:, 16:17],
                            scalar2=None, op0=ALU.is_ge),
                            waits=[("DVE", pv)], inc="DVE")
                        pv = p.add("vector", lambda e: e.tensor_tensor(
                            out=rbuf[:, 32:40], in0=eiota[:, :],
                            in1=rbuf[:, 24:32], op=ALU.mult),
                            waits=[("DVE", pv)], inc="DVE")
                        pv = p.add("vector", lambda e: e.tensor_scalar(
                            out=rbuf[:, 24:32], in0=rbuf[:, 24:32],
                            scalar1=-99.0, scalar2=99.0, op0=ALU.mult,
                            op1=ALU.add),
                            waits=[("DVE", pv)], inc="DVE")
                        pv = p.add("vector", lambda e: e.tensor_tensor(
                            out=rbuf[:, 24:32], in0=rbuf[:, 24:32],
                            in1=rbuf[:, 32:40], op=ALU.add),
                            waits=[("DVE", pv)], inc="DVE")
                        pv = p.add("vector", lambda e, tile=tile: e.tensor_reduce(
                            out=eT[:, tile, 0:1], in_=rbuf[:, 24:32], axis=AXX,
                            op=ALU.min),
                            waits=[("DVE", pv)], inc="DVE")
                        pv = p.add("vector", lambda e, wv=wv: e.tensor_scalar(
                            out=rbuf[:, 24:32], in0=wv, scalar1=0.0,
                            scalar2=None, op0=ALU.is_gt),
                            waits=[("DVE", pv)], inc="DVE")
                        pv = p.add("vector", lambda e: e.tensor_tensor(
                            out=rbuf[:, 32:40], in0=eiota[:, :],
                            in1=rbuf[:, 24:32], op=ALU.mult),
                            waits=[("DVE", pv)], inc="DVE")
                        pv = p.add("vector", lambda e: e.reduce_sum(
                            out=rbuf[:, 17:18], in_=rbuf[:, 32:40], axis=AXX),
                            waits=[("DVE", pv)], inc="DVE")
                        d_w = p.add("vector", lambda e, tile=tile: e.tensor_tensor(
                            out=eT[:, tile, 1:2], in0=rbuf[:, 17:18],
                            in1=eT[:, tile, 0:1], op=ALU.subtract),
                            waits=[("DVE", pv)], inc="DVE")
                    tw_end = None
                    for bb in range(4):
                        w = [("ACT", w_done[tc]), ("DVE", d_w)]
                        if tc >= 2 and bb == 0:
                            w.append(("DVE", wt_ev[tc - 2]))
                        tw_end = p.add(
                            "tensor",
                            lambda e, tc=tc, bb=bb, pb=pW[tc % 2]: e.transpose(
                                pb[0:16, 128 * bb:128 * bb + 128],
                                w16[:, tc % 2, bb, :], ident[:, :]),
                            waits=w,
                            inc="PE" if bb == 3 else None,
                        )
                    wt_ev[tc] = p.add(
                        "vector",
                        lambda e, tc=tc, pb=pW[tc % 2]: e.tensor_copy(
                            out=wT_sb[0:9, 512 * tc:512 * tc + 512],
                            in_=pb[0:9, :]),
                        waits=[("PE", tw_end)],
                        inc="DVE",
                    )
                p.dma("gpsimd", wTd.ap()[:, :], wT_sb[:, :], "CHS0",
                      waits=[("DVE", wt_ev[7])])
                pv = p.add("vector", lambda e: e.tensor_scalar(
                    out=indT[:, :], in0=wT_sb[0:8, :], scalar1=0.0,
                    scalar2=None, op0=ALU.is_gt),
                    waits=[("DVE", wt_ev[7])], inc="DVE")
                pv = p.add("vector", lambda e: e.tensor_tensor_scan(
                    out=posb[:, :], data0=indT[:, :], data1=indT[:, :],
                    initial=0.0, op0=ALU.add, op1=ALU.max),
                    waits=[("DVE", pv)], inc="DVE")
                pv = p.add("vector", lambda e: e.tensor_tensor(
                    out=posb[:, :], in0=posb[:, :], in1=indT[:, :],
                    op=ALU.subtract),
                    waits=[("DVE", pv)], inc="DVE")
                pv = p.add("vector", lambda e: e.tensor_scalar(
                    out=indT[:, :], in0=indT[:, :], scalar1=-BIGS,
                    scalar2=BIGS, op0=ALU.mult, op1=ALU.add),
                    waits=[("DVE", pv)], inc="DVE")
                d_pe = p.add("vector", lambda e: e.tensor_tensor(
                    out=posb[:, :], in0=posb[:, :], in1=indT[:, :],
                    op=ALU.add),
                    waits=[("DVE", pv)], inc="DVE")
                tev = {}
                for b in range(NTT):
                    w = [("DVE", d_pe)]
                    if b >= 2:
                        w.append(("DVE", tev[b - 2]))
                    pe_t = p.add(
                        "tensor",
                        lambda e, b=b, pb=pT2[b % 2]: e.transpose(
                            pb[:, 0:8], posb[0:8, 128 * b:128 * b + 128],
                            ident[0:8, 0:8]),
                        waits=w,
                        inc="PE",
                    )
                    pe_t2 = p.add(
                        "tensor",
                        lambda e, b=b, pb=pT2[b % 2]: e.transpose(
                            pb[:, 8:16], wT_sb[0:8, 128 * b:128 * b + 128],
                            ident[0:8, 0:8]),
                        inc="PE",
                    )
                    tev[b] = p.add(
                        "vector",
                        lambda e, b=b, pb=pT2[b % 2]: e.tensor_copy(
                            out=posT_f[:, b, :], in_=pb[:, 0:8]),
                        waits=[("PE", pe_t2)],
                        inc="DVE",
                    )
                    tev[b] = p.add(
                        "vector",
                        lambda e, b=b, pb=pT2[b % 2]: e.tensor_copy(
                            out=wTt_f[:, b, :], in_=pb[:, 8:16]),
                        inc="DVE",
                    )
                d_pti = p.add("vector", lambda e: e.tensor_copy(
                    out=posT_i[:, :, :], in_=posT_f[:, :, :]),
                    waits=[("DVE", tev[NTT - 1])], inc="DVE")
                pv = d_pti
                for tile in range(NTT):
                    for r in range(2):
                        pv = p.add("vector", lambda e, tile=tile, r=r:
                            e.tensor_scalar(
                                out=rbuf[:, 24:32], in0=eiota[:, :],
                                scalar1=eT[:, tile, r:r + 1], scalar2=None,
                                op0=ALU.is_equal),
                            waits=[("DVE", pv)], inc="DVE")
                        pv = p.add("vector", lambda e, tile=tile: e.tensor_tensor(
                            out=rbuf[:, 32:40], in0=posT_f[:, tile, :],
                            in1=rbuf[:, 24:32], op=ALU.mult),
                            waits=[("DVE", pv)], inc="DVE")
                        pv = p.add("vector", lambda e: e.reduce_sum(
                            out=rbuf[:, 16:17], in_=rbuf[:, 32:40], axis=AXX),
                            waits=[("DVE", pv)], inc="DVE")
                        pv = p.add("vector", lambda e: e.tensor_scalar(
                            out=rbuf[:, 18:19], in0=rbuf[:, 16:17],
                            scalar1=CAP - 0.5, scalar2=None, op0=ALU.is_gt),
                            waits=[("DVE", pv)], inc="DVE")
                        pv = p.add("vector", lambda e, tile=tile, r=r:
                            e.scalar_tensor_tensor(
                                out=rbuf[:, 19:20], in0=eT[:, tile, r:r + 1],
                                scalar=float(CAP), in1=rbuf[:, 16:17],
                                op0=ALU.mult, op1=ALU.add),
                            waits=[("DVE", pv)], inc="DVE")
                        pv = p.add("vector", lambda e, tile=tile, r=r:
                            e.scalar_tensor_tensor(
                                out=gidx_f[:, tile, r:r + 1],
                                in0=rbuf[:, 18:19], scalar=BIGS,
                                in1=rbuf[:, 19:20], op0=ALU.mult, op1=ALU.add),
                            waits=[("DVE", pv)], inc="DVE")
                d_gi = p.add("vector", lambda e: e.tensor_copy(
                    out=gidx_i[:, :, :], in_=gidx_f[:, :, :]),
                    waits=[("DVE", pv)], inc="DVE")
                p.add("gpsimd", lambda e: e.nop(), waits=p.st_waits())
                p.run()

            with contextlib.ExitStack() as ps:
                wgraw = ps.enter_context(nc.sbuf_tensor("wgraw", [128, 2, D], F32))
                wuraw = ps.enter_context(nc.sbuf_tensor("wuraw", [128, 2, D], F32))
                wgTh2 = ps.enter_context(
                    nc.sbuf_tensor("wgTh2", [128, 2, ND, 128], BF16))
                wgTl2 = ps.enter_context(
                    nc.sbuf_tensor("wgTl2", [128, 2, ND, 128], BF16))
                wuTh2 = ps.enter_context(
                    nc.sbuf_tensor("wuTh2", [128, 2, ND, 128], BF16))
                wuTl2 = ps.enter_context(
                    nc.sbuf_tensor("wuTl2", [128, 2, ND, 128], BF16))
                wb = ps.enter_context(nc.sbuf_tensor("wb", [128, T], F32))
                t1 = ps.enter_context(nc.sbuf_tensor("t1", [128, 1, 512], F32))
                t2 = ps.enter_context(nc.sbuf_tensor("t2", [128, 1, 512], F32))
                t2h = ps.enter_context(nc.sbuf_tensor("t2h", [128, 2, 512], BF16))
                t2l = ps.enter_context(nc.sbuf_tensor("t2l", [128, 2, 512], BF16))
                h2t = ps.enter_context(nc.sbuf_tensor("h2t", [128, 2, D], F32))
                xwt = ps.enter_context(nc.sbuf_tensor("xwt", [128, 2, D], F32))
                pWT = [ps.enter_context(nc.psum_tensor(f"p6w{i}", [128, 512], F32))
                       for i in range(2)]
                pg = [ps.enter_context(nc.psum_tensor(f"p6g{i}", [128, 512], F32))
                      for i in range(2)]
                pu = [ps.enter_context(nc.psum_tensor(f"p6u{i}", [128, 512], F32))
                      for i in range(2)]
                p = Phase(nc, sems, cnt)
                ldwb = p.dma("sync", wb[:, :],
                             dap(wTd, 8 * T, (0, 128), (1, T)), "CHC")
                sc_steps = _scatter_steps(p, range(0, 4), h2t, xwt, h2tmd,
                                          xar, xwar, posT_i, wTt_f, d_pti,
                                          d_gi)
                sc_it = iter(sc_steps)
                mm_end = {}
                silu_a = {}
                dh1 = {}
                hid1 = {}
                st_done = {}
                wt_ev = {}
                for i in range(22):
                    w = [("PE", mm_end[(i - 2, 7)])] if i >= 2 else []
                    srcg = wg[128 * i:128 * i + 128, :]
                    srcu = wu[128 * i:128 * i + 128, :]
                    p.dma("sync", wgraw[:, i % 2, :], srcg,
                          f"CHL{i % 2}", waits=w)
                    ldu = p.dma("sync", wuraw[:, i % 2, :], srcu,
                                f"CHL{i % 2}", waits=w)
                    for k in range(4):
                        raw = (wgraw if k < 2 else wuraw)[:, i % 2, :]
                        kg = i * 4 + k
                        t_end = None
                        for dd in range(4):
                            d = 4 * (k % 2) + dd
                            w = [ldu]
                            if kg >= 2 and dd == 0:
                                w.append(("DVE", wt_ev[kg - 2]))
                            t_end = p.add(
                                "tensor",
                                lambda e, raw=raw, d=d, dd=dd, pb=pWT[kg % 2]:
                                    e.transpose(
                                        pb[:, 128 * dd:128 * dd + 128],
                                        raw[:, 128 * d:128 * d + 128],
                                        ident[:, :]),
                                waits=w,
                                inc="PE" if dd == 3 else None,
                            )
                        dsth2 = (wgTh2 if k < 2 else wuTh2)
                        dstl2 = (wgTl2 if k < 2 else wuTl2)
                        half = k % 2
                        w = [("PE", t_end)]
                        if i >= 2:
                            w.append(("PE", mm_end[(i - 2, 7)]))
                        s_hi = p.add(
                            "scalar",
                            lambda e, dstT=dsth2, i=i, half=half, pb=pWT[kg % 2]:
                                e.copy(
                                    out=dstT[:, i % 2, 4 * half:4 * half + 4, :],
                                    in_=pb[:, :].rearrange(
                                        "p (a b) -> p a b", a=4)),
                            waits=w,
                            inc="ACT",
                        )
                        wt_ev[kg] = p.add(
                            "vector",
                            lambda e, dh2=dsth2, dl2=dstl2, i=i, half=half,
                                   pb=pWT[kg % 2]:
                                e.scalar_tensor_tensor(
                                    out=dl2[:, i % 2, 4 * half:4 * half + 4, :],
                                    in0=pb[:, :].rearrange(
                                        "p (a b) -> p a b", a=4),
                                    scalar=1.0,
                                    in1=dh2[:, i % 2, 4 * half:4 * half + 4, :],
                                    op0=ALU.mult, op1=ALU.subtract),
                            waits=[("ACT", s_hi)],
                            inc="DVE",
                        )
                    wt_last = wt_ev[i * 4 + 3]
                    for c in range(8):
                        gc = i * 8 + c
                        np3 = 0
                        for (wa, hb) in ((0, 0), (0, 1), (1, 0)):
                            wsrc = (wgTh2 if wa == 0 else wgTl2)
                            hsrc = (h2h if hb == 0 else h2l)
                            for d in range(ND):
                                w = [("DVE", wt_last)]
                                if gc >= 2 and np3 == 0:
                                    w.append(("ACT", silu_a.get(gc - 2)))
                                p.add(
                                    "tensor",
                                    lambda e, i=i, d=d, c=c, ws=wsrc, hs=hsrc,
                                           st=(np3 == 0), sp=(np3 == 23),
                                           pb=pg[c % 2]: e.matmul(
                                        pb[:, :], ws[:, i % 2, d, :],
                                        hs[:, d, 512 * c:512 * c + 512],
                                        start=st, stop=sp),
                                    waits=w,
                                )
                                np3 += 1
                        np3 = 0
                        for (wa, hb) in ((0, 0), (0, 1), (1, 0)):
                            wsrc = (wuTh2 if wa == 0 else wuTl2)
                            hsrc = (h2h if hb == 0 else h2l)
                            for d in range(ND):
                                w = []
                                if gc >= 2 and np3 == 0:
                                    w.append(("DVE", dh1.get(gc - 2)))
                                mm_end[(i, c)] = p.add(
                                    "tensor",
                                    lambda e, i=i, d=d, c=c, ws=wsrc, hs=hsrc,
                                           st=(np3 == 0), sp=(np3 == 23),
                                           pb=pu[c % 2]: e.matmul(
                                        pb[:, :], ws[:, i % 2, d, :],
                                        hs[:, d, 512 * c:512 * c + 512],
                                        start=st, stop=sp),
                                    waits=w,
                                    inc="PE" if np3 == 23 else None,
                                )
                                np3 += 1
                        w = [("PE", mm_end[(i, c)])]
                        if gc >= 1:
                            w.append(("DVE", dh1.get(gc - 1)))
                        silu_a[gc] = p.add(
                            "scalar",
                            lambda e, c=c, pb=pg[c % 2]: e.activation(
                                out=t1[:, 0, :], in_=pb[:, :],
                                func=AF.Silu),
                            waits=w,
                            inc="ACT",
                        )
                        w = [("ACT", silu_a[gc])]
                        if gc >= 1:
                            w.append(("DVE", hid1.get(gc - 1)))
                        if gc >= 2:
                            w.append(st_done.get(gc - 2))
                        dh1[gc] = p.add(
                            "vector",
                            lambda e, c=c, pb=pu[c % 2]: e.tensor_tensor(
                                out=t2[:, 0, :], in0=t1[:, 0, :],
                                in1=pb[:, :], op=ALU.mult),
                            waits=w,
                            inc="DVE",
                        )
                        w = [("DVE", dh1[gc]), ldwb]
                        hid0 = p.add(
                            "vector",
                            lambda e, c=c: e.tensor_tensor(
                                out=t2[:, 0, :], in0=t2[:, 0, :],
                                in1=wb[:, 512 * c:512 * c + 512],
                                op=ALU.mult),
                            waits=w,
                            inc="DVE",
                        )
                        hhi = p.add(
                            "scalar",
                            lambda e, gc=gc: e.copy(
                                out=t2h[:, gc % 2, :], in_=t2[:, 0, :]),
                            waits=[("DVE", hid0)] + (
                                [st_done[gc - 2]] if gc >= 2 else []),
                            inc="ACT",
                        )
                        hid1[gc] = p.add(
                            "vector",
                            lambda e, c=c, gc=gc: e.tensor_tensor(
                                out=t2l[:, gc % 2, :], in0=t2[:, 0, :],
                                in1=t2h[:, gc % 2, :], op=ALU.subtract),
                            waits=[("ACT", hhi)],
                            inc="DVE",
                        )
                        p.dma(
                            "gpsimd",
                            dap(hidd_sh, (128 * i) * T + 512 * c,
                                (T, 128), (1, 512)),
                            t2h[:, gc % 2, :], f"CHS{gc % 2}",
                            waits=[("ACT", hhi), ("DVE", hid1[gc])])
                        st_done[gc] = p.dma(
                            "gpsimd",
                            dap(hidd_sl, (128 * i) * T + 512 * c,
                                (T, 128), (1, 512)),
                            t2l[:, gc % 2, :], f"CHS{gc % 2}",
                            waits=[("DVE", hid1[gc])])
                        if gc % 2 == 0:
                            nxt = next(sc_it, None)
                            if nxt:
                                nxt()
                for nxt in sc_it:
                    nxt()
                p.add("gpsimd", lambda e: e.nop(),
                      waits=p.st_waits() + [(f"CHX{e2}", cnt[f"CHX{e2}"])
                                            for e2 in range(0, 4)])
                p.run()

        with contextlib.ExitStack() as ps:
            hidch = ps.enter_context(
                nc.sbuf_tensor("hidcBh", [128, 2, 22, 128], BF16))
            hidcl = ps.enter_context(
                nc.sbuf_tensor("hidcBl", [128, 2, 22, 128], BF16))
            draw = ps.enter_context(nc.sbuf_tensor("drawB", [128, 2, FS], F32))
            wdTh = ps.enter_context(nc.sbuf_tensor("wdTBh", [128, 22, D], BF16))
            wdTl = ps.enter_context(nc.sbuf_tensor("wdTBl", [128, 22, D], BF16))
            dstage = ps.enter_context(nc.sbuf_tensor("dstageB", [128, 2, D], F32))
            h2t = ps.enter_context(nc.sbuf_tensor("h2tB", [128, 2, D], F32))
            xwt = ps.enter_context(nc.sbuf_tensor("xwtB", [128, 2, D], F32))
            pWT = [ps.enter_context(nc.psum_tensor(f"p8w{i}", [128, 512], F32))
                   for i in range(2)]
            pD = [ps.enter_context(nc.psum_tensor(f"p8d{i}", [128, 512], F32))
                  for i in range(2)]
            p = Phase(nc, sems, cnt)
            sc_steps = _scatter_steps(p, range(4, 8), h2t, xwt, h2tmd,
                                      xar, xwar, posT_i, wTt_f, d_pti, d_gi)
            sc_it = iter(sc_steps)
            wt_ev = {}
            hh_t_end = {}
            kidx = 0
            for hh in range(ND):
                w = []
                if hh >= 2:
                    w.append(("PE", hh_t_end[hh - 2]))
                ldh = p.dma("sync", draw[:, hh % 2, :],
                            wd[128 * hh:128 * hh + 128, :],
                            f"CHL{hh % 2}", waits=w)
                for kb in range(6):
                    t_end = None
                    f0 = 4 * kb
                    nfk = min(4, 22 - f0)
                    for dd in range(nfk):
                        f = f0 + dd
                        w = [ldh]
                        if kidx >= 2 and dd == 0:
                            w.append(("DVE", wt_ev[kidx - 2]))
                        t_end = p.add(
                            "tensor",
                            lambda e, hh=hh, f=f, dd=dd, pb=pWT[kidx % 2]:
                                e.transpose(pb[:, 128 * dd:128 * dd + 128],
                                            draw[:, hh % 2,
                                                 128 * f:128 * f + 128],
                                            ident[:, :]),
                            waits=w,
                            inc="PE" if dd == nfk - 1 else None,
                        )
                    hh_t_end[hh] = t_end
                    a_hi = p.add(
                        "scalar",
                        lambda e, hh=hh, f0=f0, nfk=nfk, pb=pWT[kidx % 2]:
                            e.copy(
                                out=wdTh[:, f0:f0 + nfk,
                                         128 * hh:128 * hh + 128],
                                in_=pb[:, 0:nfk * 128].rearrange(
                                    "p (a b) -> p a b", a=nfk)),
                        waits=[("PE", t_end)],
                        inc="ACT",
                    )
                    wt_ev[kidx] = p.add(
                        "vector",
                        lambda e, hh=hh, f0=f0, nfk=nfk, pb=pWT[kidx % 2]:
                            e.scalar_tensor_tensor(
                                out=wdTl[:, f0:f0 + nfk,
                                         128 * hh:128 * hh + 128],
                                in0=pb[:, 0:nfk * 128].rearrange(
                                    "p (a b) -> p a b", a=nfk),
                                scalar=1.0,
                                in1=wdTh[:, f0:f0 + nfk,
                                         128 * hh:128 * hh + 128],
                                op0=ALU.mult, op1=ALU.subtract),
                        waits=[("ACT", a_hi)],
                        inc="DVE",
                    )
                    kidx += 1
            prep_done = wt_ev[kidx - 1]
            mm_end = {}
            dv_ev = {}
            st_done = {}
            for m in range(NTT):
                w = []
                if m >= 2:
                    w.append(("PE", mm_end[(m - 2, 1)]))
                p.dma(
                    "sync", hidch[:, m % 2, :, :],
                    dap(hidd_sh, 128 * m, (T, 128), (128 * T, 22), (1, 128)),
                    f"CHL{2 + m % 2}", waits=w)
                ldc = p.dma(
                    "sync", hidcl[:, m % 2, :, :],
                    dap(hidd_sl, 128 * m, (T, 128), (128 * T, 22), (1, 128)),
                    f"CHL{2 + m % 2}", waits=w)
                for hc in range(2):
                    gmc = 2 * m + hc
                    np3 = 0
                    for (ha, hb) in ((0, 0), (0, 1), (1, 0)):
                        for f in range(22):
                            w = [ldc, ("DVE", prep_done)]
                            if gmc >= 2 and np3 == 0:
                                idx = gmc - 2
                                w.append(("DVE", dv_ev[(idx // 2, idx % 2)]))
                            hsrc2 = (hidch if ha == 0 else hidcl)
                            wsrc2 = (wdTh if hb == 0 else wdTl)
                            mm_end[(m, hc)] = p.add(
                                "tensor",
                                lambda e, m=m, hc=hc, f=f, hs=hsrc2, ws=wsrc2,
                                       st=(np3 == 0), pb=pD[gmc % 2]:
                                    e.matmul(pb[:, :], hs[:, m % 2, f, :],
                                             ws[:, f, 512 * hc:512 * hc + 512],
                                             start=st, stop=False),
                                waits=w,
                                inc=None,
                            )
                            np3 += 1
                    mm_end[(m, hc)] = p.add(
                        "tensor",
                        lambda e, m=m, hc=hc, pb=pD[gmc % 2]:
                            e.matmul(pb[:, :], hidcl[:, m % 2, 21, :],
                                     wdTl[:, 21, 512 * hc:512 * hc + 512],
                                     start=False, stop=True),
                        inc="PE",
                    )
                    w = [("PE", mm_end[(m, hc)])]
                    if gmc >= 2:
                        idx = gmc - 2
                        w.append(st_done[(idx // 2, idx % 2)])
                    dv_ev[(m, hc)] = p.add(
                        "vector",
                        lambda e, m=m, hc=hc, pb=pD[gmc % 2]: e.tensor_copy(
                            out=dstage[:, m % 2, 512 * hc:512 * hc + 512],
                            in_=pb[:, :]),
                        waits=w,
                        inc="DVE",
                    )
                    st_done[(m, hc)] = p.dma(
                        "gpsimd",
                        shoutd.ap()[128 * m:128 * m + 128,
                                    512 * hc:512 * hc + 512],
                        dstage[:, m % 2, 512 * hc:512 * hc + 512],
                        f"CHS{gmc % 2}",
                        waits=[("DVE", dv_ev[(m, hc)])])
                    nxt = next(sc_it, None)
                    if nxt:
                        nxt()
            for nxt in sc_it:
                nxt()
            p.add("gpsimd", lambda e: e.nop(),
                  waits=p.st_waits() + [(f"CHX{e2}", cnt[f"CHX{e2}"])
                                        for e2 in range(4, 8)])
            p.run()

        with contextlib.ExitStack() as ps:
            ast_x = ps.enter_context(nc.sbuf_tensor("ast_x", [128, D], F32))
            ast_w = ps.enter_context(nc.sbuf_tensor("ast_w", [128, D], F32))
            xfmh = ps.enter_context(nc.sbuf_tensor("xfmh", [128, ND, 512], BF16))
            xfml = ps.enter_context(nc.sbuf_tensor("xfml", [128, ND, 512], BF16))
            xwfmh = ps.enter_context(nc.sbuf_tensor("xwfmh", [128, ND, 512], BF16))
            xwfml = ps.enter_context(nc.sbuf_tensor("xwfml", [128, ND, 512], BF16))
            wgTh = ps.enter_context(nc.sbuf_tensor("wgTh", [128, 2, ND, 128], BF16))
            wgTl = ps.enter_context(nc.sbuf_tensor("wgTl", [128, 2, ND, 128], BF16))
            wuTh = ps.enter_context(nc.sbuf_tensor("wuTh", [128, 2, ND, 128], BF16))
            wuTl = ps.enter_context(nc.sbuf_tensor("wuTl", [128, 2, ND, 128], BF16))
            wgraw = ps.enter_context(nc.sbuf_tensor("wgrawe", [128, 2, D], F32))
            wuraw = ps.enter_context(nc.sbuf_tensor("wurawe", [128, 2, D], F32))
            t1e = ps.enter_context(nc.sbuf_tensor("t1e", [128, 2, 512], F32))
            t2e = ps.enter_context(nc.sbuf_tensor("t2e", [128, 2, 512], F32))
            hidqh = ps.enter_context(nc.sbuf_tensor("hidqh", [128, 2, 11, 512], BF16))
            hidql = ps.enter_context(nc.sbuf_tensor("hidql", [128, 2, 11, 512], BF16))
            wdTeh = ps.enter_context(nc.sbuf_tensor("wdTeh", [128, 11, D], BF16))
            wdTel = ps.enter_context(nc.sbuf_tensor("wdTel", [128, 11, D], BF16))
            drawe = ps.enter_context(nc.sbuf_tensor("drawe", [128, FE], F32))
            dst2 = ps.enter_context(nc.sbuf_tensor("dst2", [128, 2, 512], F32))
            pWT = [ps.enter_context(nc.psum_tensor(f"pew{i}", [128, 512], F32))
                   for i in range(2)]
            pg = [ps.enter_context(nc.psum_tensor(f"peg{i}", [128, 512], F32))
                  for i in range(2)]
            pu = [ps.enter_context(nc.psum_tensor(f"peu{i}", [128, 512], F32))
                  for i in range(2)]
            pD = [ps.enter_context(nc.psum_tensor(f"ped{i}", [128, 512], F32))
                  for i in range(2)]
            for ee in range(NE):
                p = Phase(nc, sems, cnt)
                scw = (f"CHX{ee}", cnt[f"CHX{ee}"])

                def _split_evac(pb, nfk, out_h, out_l, t_end, extra=()):
                    a_hi = p.add(
                        "scalar",
                        lambda e, pb=pb, nfk=nfk, out_h=out_h: e.copy(
                            out=out_h,
                            in_=pb[:, 0:nfk * 128].rearrange(
                                "p (a b) -> p a b", a=nfk)),
                        waits=[("PE", t_end)] + list(extra),
                        inc="ACT",
                    )
                    return p.add(
                        "vector",
                        lambda e, pb=pb, nfk=nfk, out_h=out_h, out_l=out_l:
                            e.scalar_tensor_tensor(
                                out=out_l,
                                in0=pb[:, 0:nfk * 128].rearrange(
                                    "p (a b) -> p a b", a=nfk),
                                scalar=1.0, in1=out_h,
                                op0=ALU.mult, op1=ALU.subtract),
                        waits=[("ACT", a_hi)],
                        inc="DVE",
                    )

                wt_ev = {}
                hh_t_end = {}
                kidx = 0
                for hh in range(ND):
                    w = []
                    if hh >= 1:
                        w.append(("PE", hh_t_end[hh - 1]))
                    ldh = p.dma("sync", drawe[:, :],
                                edown[ee, 128 * hh:128 * hh + 128, :],
                                f"CHL{hh % 2}", waits=w)
                    for kb in range(3):
                        t_end = None
                        f0 = 4 * kb
                        nfk = min(4, 11 - f0)
                        for dd in range(nfk):
                            f = f0 + dd
                            w = [ldh]
                            if kidx >= 2 and dd == 0:
                                w.append(("DVE", wt_ev[kidx - 2]))
                            t_end = p.add(
                                "tensor",
                                lambda e, f=f, dd=dd, pb=pWT[kidx % 2]:
                                    e.transpose(pb[:, 128 * dd:128 * dd + 128],
                                                drawe[:, 128 * f:128 * f + 128],
                                                ident[:, :]),
                                waits=w,
                                inc="PE" if dd == nfk - 1 else None,
                            )
                        hh_t_end[hh] = t_end
                        wt_ev[kidx] = _split_evac(
                            pWT[kidx % 2], nfk,
                            wdTeh[:, f0:f0 + nfk, 128 * hh:128 * hh + 128],
                            wdTel[:, f0:f0 + nfk, 128 * hh:128 * hh + 128],
                            t_end)
                        kidx += 1
                dprep = wt_ev[kidx - 1]
                pwt_hist = dict(wt_ev)
                pwt_last_ev = dprep
                pwt_kidx = kidx
                silu_a = {}
                dh1 = {}
                hid_last = {}
                mm_gu = {}
                mm_d = {}
                dv_ev = {}
                st_done = {}
                gu_end = {}
                at_end = {}
                aev = {}
                for c in range(NSC):
                    for q in range(4):
                        qg = 4 * c + q
                        base = ee * CAP + 512 * c + 128 * q
                        w = [scw]
                        if qg >= 1:
                            w.append(("PE", at_end[qg - 1]))
                        if q == 0 and c >= 1:
                            w.append(("PE", gu_end[c - 1]))
                        lax = p.dma("sync", ast_x[:, :],
                                    dap(xar, base * D, (D, 128), (1, D)),
                                    f"CHL{2 + qg % 2}", waits=w)
                        law = p.dma("sync", ast_w[:, :],
                                    dap(xwar, base * D, (D, 128), (1, D)),
                                    f"CHL{2 + qg % 2}", waits=w)
                        for half in range(2):
                            for src_i in range(2):
                                t_end = None
                                for dd in range(4):
                                    d = 4 * half + dd
                                    buf = ast_x if src_i == 0 else ast_w
                                    w = [law]
                                    if dd == 0:
                                        w.append(("DVE",
                                                  pwt_hist.get(pwt_kidx - 2)))
                                    t_end = p.add(
                                        "tensor",
                                        lambda e, buf=buf, d=d, dd=dd,
                                               pb=pWT[pwt_kidx % 2]:
                                            e.transpose(
                                                pb[:, 128 * dd:128 * dd + 128],
                                                buf[:, 128 * d:128 * d + 128],
                                                ident[:, :]),
                                        waits=w,
                                        inc="PE" if dd == 3 else None,
                                    )
                                dsth = (xfmh if src_i == 0 else xwfmh)
                                dstl = (xfml if src_i == 0 else xwfml)
                                ev = _split_evac(
                                    pWT[pwt_kidx % 2], 4,
                                    dsth[:, 4 * half:4 * half + 4,
                                         128 * q:128 * q + 128],
                                    dstl[:, 4 * half:4 * half + 4,
                                         128 * q:128 * q + 128],
                                    t_end)
                                pwt_hist[pwt_kidx] = ev
                                pwt_last_ev = ev
                                pwt_kidx += 1
                        at_end[qg] = t_end
                        aev[qg] = pwt_last_ev
                    a_last = aev[4 * c + 3]
                    for i in range(11):
                        ig = 11 * c + i
                        w = []
                        if i >= 2:
                            w.append(("PE", mm_gu[(c, i - 2)]))
                        elif c >= 1:
                            w.append(("PE", mm_gu[(c - 1, 9 + i)]))
                        p.dma("sync", wgraw[:, i % 2, :],
                              egu[ee, 128 * i:128 * i + 128, :],
                              f"CHL{i % 2}", waits=w)
                        ldu = p.dma("sync", wuraw[:, i % 2, :],
                                    egu[ee, FE + 128 * i:FE + 128 * i + 128, :],
                                    f"CHL{i % 2}", waits=w)
                        for k in range(4):
                            raw = (wgraw if k < 2 else wuraw)[:, i % 2, :]
                            t_end = None
                            for dd in range(4):
                                d = 4 * (k % 2) + dd
                                w = [ldu]
                                if dd == 0:
                                    w.append(("DVE",
                                              pwt_hist.get(pwt_kidx - 2)))
                                t_end = p.add(
                                    "tensor",
                                    lambda e, raw=raw, d=d, dd=dd,
                                           pb=pWT[pwt_kidx % 2]:
                                        e.transpose(
                                            pb[:, 128 * dd:128 * dd + 128],
                                            raw[:, 128 * d:128 * d + 128],
                                            ident[:, :]),
                                    waits=w,
                                    inc="PE" if dd == 3 else None,
                                )
                            half = k % 2
                            dsth = (wgTh if k < 2 else wuTh)
                            dstl = (wgTl if k < 2 else wuTl)
                            w2 = []
                            if i >= 2:
                                w2.append(("PE", mm_gu[(c, i - 2)]))
                            elif c >= 1:
                                w2.append(("PE", mm_gu[(c - 1, 9 + i)]))
                            ev = _split_evac(
                                pWT[pwt_kidx % 2], 4,
                                dsth[:, i % 2, 4 * half:4 * half + 4, :],
                                dstl[:, i % 2, 4 * half:4 * half + 4, :],
                                t_end, extra=w2)
                            pwt_hist[pwt_kidx] = ev
                            pwt_last_ev = ev
                            pwt_kidx += 1
                        wt_last = pwt_last_ev
                        np3 = 0
                        for (wa, xa2) in ((0, 0), (0, 1), (1, 0)):
                            wsrc = (wgTh if wa == 0 else wgTl)
                            xsrc = (xfmh if xa2 == 0 else xfml)
                            for d in range(ND):
                                w = [("DVE", wt_last), ("DVE", a_last)]
                                if ig >= 2 and np3 == 0:
                                    w.append(("ACT", silu_a.get(ig - 2)))
                                p.add(
                                    "tensor",
                                    lambda e, i=i, d=d, ws=wsrc, xs=xsrc,
                                           st=(np3 == 0), sp=(np3 == 23),
                                           pb=pg[i % 2]: e.matmul(
                                        pb[:, :], ws[:, i % 2, d, :],
                                        xs[:, d, :], start=st, stop=sp),
                                    waits=w,
                                )
                                np3 += 1
                        np3 = 0
                        for (wa, xa2) in ((0, 0), (0, 1), (1, 0)):
                            wsrc = (wuTh if wa == 0 else wuTl)
                            xsrc = (xwfmh if xa2 == 0 else xwfml)
                            for d in range(ND):
                                w = []
                                if ig >= 2 and np3 == 0:
                                    w.append(("DVE", dh1.get(ig - 2)))
                                mm_gu[(c, i)] = p.add(
                                    "tensor",
                                    lambda e, i=i, d=d, ws=wsrc, xs=xsrc,
                                           st=(np3 == 0), sp=(np3 == 23),
                                           pb=pu[i % 2]: e.matmul(
                                        pb[:, :], ws[:, i % 2, d, :],
                                        xs[:, d, :], start=st, stop=sp),
                                    waits=w,
                                    inc="PE" if np3 == 23 else None,
                                )
                                np3 += 1
                        w = [("PE", mm_gu[(c, i)])]
                        if ig >= 2:
                            w.append(("DVE", dh1.get(ig - 2)))
                        silu_a[ig] = p.add(
                            "scalar",
                            lambda e, i=i, pb=pg[i % 2]: e.activation(
                                out=t1e[:, i % 2, :], in_=pb[:, :],
                                func=AF.Silu),
                            waits=w,
                            inc="ACT",
                        )
                        w = [("ACT", silu_a[ig])]
                        if c >= 2:
                            w.append(("PE", mm_d[(c - 2, 7)]))
                        dmul = p.add(
                            "vector",
                            lambda e, i=i, pb=pu[i % 2]: e.tensor_tensor(
                                out=t2e[:, i % 2, :], in0=t1e[:, i % 2, :],
                                in1=pb[:, :], op=ALU.mult),
                            waits=w,
                            inc="DVE",
                        )
                        h_hi = p.add(
                            "scalar",
                            lambda e, c=c, i=i: e.copy(
                                out=hidqh[:, c % 2, i, :],
                                in_=t2e[:, i % 2, :]),
                            waits=[("DVE", dmul)],
                            inc="ACT",
                        )
                        dh1[ig] = p.add(
                            "vector",
                            lambda e, c=c, i=i: e.tensor_tensor(
                                out=hidql[:, c % 2, i, :],
                                in0=t2e[:, i % 2, :],
                                in1=hidqh[:, c % 2, i, :], op=ALU.subtract),
                            waits=[("ACT", h_hi)],
                            inc="DVE",
                        )
                    hid_last[c] = dh1[11 * c + 10]
                    gu_end[c] = mm_gu[(c, 10)]
                    for mq in range(4):
                        for hc in range(2):
                            gmc = 8 * c + 2 * mq + hc
                            np3 = 0
                            for (ha, hb) in ((0, 0), (0, 1), (1, 0), (1, 1)):
                                hsrc = (hidqh if ha == 0 else hidql)
                                wsrc = (wdTeh if hb == 0 else wdTel)
                                for f in range(11):
                                    w = [("DVE", hid_last[c]),
                                         ("DVE", dprep)]
                                    if gmc >= 2 and np3 == 0:
                                        w.append(("DVE", dv_ev.get(gmc - 2)))
                                    mm_d[(c, 2 * mq + hc)] = p.add(
                                        "tensor",
                                        lambda e, c=c, mq=mq, hc=hc, f=f,
                                               hs=hsrc, ws=wsrc,
                                               st=(np3 == 0), sp=(np3 == 43),
                                               pb=pD[gmc % 2]:
                                            e.matmul(
                                                pb[:, :],
                                                hs[:, c % 2, f,
                                                   128 * mq:128 * mq + 128],
                                                ws[:, f,
                                                   512 * hc:512 * hc + 512],
                                                start=st, stop=sp),
                                        waits=w,
                                        inc="PE" if np3 == 43 else None,
                                    )
                                    np3 += 1
                            w = [("PE", mm_d[(c, 2 * mq + hc)])]
                            if gmc >= 2:
                                w.append(st_done.get(gmc - 2))
                            dv_ev[gmc] = p.add(
                                "vector",
                                lambda e, gmc=gmc, pb=pD[gmc % 2]:
                                    e.tensor_copy(
                                        out=dst2[:, gmc % 2, :], in_=pb[:, :]),
                                waits=w,
                                inc="DVE",
                            )
                            st_done[gmc] = p.dma(
                                "gpsimd",
                                dap(dar,
                                    (ee * CAP + 512 * c + 128 * mq) * D
                                    + 512 * hc,
                                    (D, 128), (1, 512)),
                                dst2[:, gmc % 2, :], f"CHS{gmc % 2}",
                                waits=[("DVE", dv_ev[gmc])])
                p.add("gpsimd", lambda e: e.nop(), waits=p.st_waits())
                p.run()

        with contextlib.ExitStack() as ps:
            x1b = ps.enter_context(nc.sbuf_tensor("x1bF", [128, 2, D], F32))
            shb = ps.enter_context(nc.sbuf_tensor("shbF", [128, 2, D], F32))
            g0b = ps.enter_context(nc.sbuf_tensor("g0bF", [128, 2, D], F32))
            g1b = ps.enter_context(nc.sbuf_tensor("g1bF", [128, 2, D], F32))
            acc = ps.enter_context(nc.sbuf_tensor("accF", [128, 2, D], F32))
            p = Phase(nc, sems, cnt)
            holdF = {}

            def _setupF(e):
                bc = e.alloc_register()
                e.reg_mov(bc, NE * CAP - 1)
                holdF["bc"] = bc
                return e.nop()

            p.add("gpsimd", _setupF)
            a1 = {}
            a3 = {}
            g_done = {}
            st_out = {}
            for tile in range(NTT):
                w = [("DVE", a1.get(tile - 2))] if tile >= 2 else []
                ldx1 = p.dma("sync", x1b[:, tile % 2, :],
                             x1d.ap()[128 * tile:128 * tile + 128, :],
                             f"CHL{tile % 2}", waits=w)
                ldsh = p.dma("sync", shb[:, tile % 2, :],
                             shoutd.ap()[128 * tile:128 * tile + 128, :],
                             f"CHL{2 + tile % 2}", waits=w)
                w2 = [("DVE", a3.get(tile - 2))] if tile >= 2 else []
                g0 = p.dma(
                    "gpsimd", g0b[:, tile % 2, :], None, f"CHX{tile % 8}",
                    waits=w2) if False else p.add(
                    "gpsimd",
                    lambda e, tile=tile: e.indirect_dma_start(
                        out=g0b[:, tile % 2, :],
                        out_offset=None,
                        in_=dar.ap(),
                        in_offset=bass.IndirectOffsetOnAxis(
                            gidx_i[:, tile, 0:1], 0),
                        bounds_check=holdF["bc"],
                        oob_is_err=False),
                    waits=w2,
                    inc=f"CHX{tile % 8}",
                    inc_n=16,
                )
                g1 = p.add(
                    "gpsimd",
                    lambda e, tile=tile: e.indirect_dma_start(
                        out=g1b[:, tile % 2, :],
                        out_offset=None,
                        in_=dar.ap(),
                        in_offset=bass.IndirectOffsetOnAxis(
                            gidx_i[:, tile, 1:2], 0),
                        bounds_check=holdF["bc"],
                        oob_is_err=False),
                    inc=f"CHX{tile % 8}",
                    inc_n=16,
                )
                g_done[tile] = (f"CHX{tile % 8}", g1)
                a1[tile] = p.add(
                    "vector",
                    lambda e, tile=tile: e.tensor_tensor(
                        out=acc[:, tile % 2, :], in0=x1b[:, tile % 2, :],
                        in1=shb[:, tile % 2, :], op=ALU.add),
                    waits=[ldx1, ldsh] + ([st_out.get(tile - 2)]
                                          if tile >= 2 else []),
                    inc="DVE",
                )
                a2 = p.add(
                    "vector",
                    lambda e, tile=tile: e.tensor_tensor(
                        out=acc[:, tile % 2, :], in0=acc[:, tile % 2, :],
                        in1=g0b[:, tile % 2, :], op=ALU.add),
                    waits=[("DVE", a1[tile]), g_done[tile]],
                    inc="DVE",
                )
                a3[tile] = p.add(
                    "vector",
                    lambda e, tile=tile: e.tensor_tensor(
                        out=acc[:, tile % 2, :], in0=acc[:, tile % 2, :],
                        in1=g1b[:, tile % 2, :], op=ALU.add),
                    waits=[("DVE", a2)],
                    inc="DVE",
                )
                st_out[tile] = p.dma(
                    "gpsimd",
                    out_t[128 * tile:128 * tile + 128, :],
                    acc[:, tile % 2, :], f"CHS{tile % 2}",
                    waits=[("DVE", a3[tile])])
            p.add("gpsimd", lambda e: e.nop(), waits=p.st_waits())
            p.run()

    return nc


_NC_CACHE = [None]


def kernel(L0_attn_x, L0_attn_W_qkv, L0_attn_b_qkv, L0_ffn_gate_w,
           L0_ffn_experts_gate_up, L0_ffn_experts_down, L0_ffn_shared_gate_w,
           L0_ffn_shared_up_w, L0_ffn_shared_down_w, L0_ffn_shared_expert_gate_w):
    x = np.ascontiguousarray(np.asarray(L0_attn_x, dtype=np.float32))
    weights = {
        "wqkv": np.ascontiguousarray(np.asarray(L0_attn_W_qkv, dtype=np.float32)),
        "bqkv": np.ascontiguousarray(np.asarray(L0_attn_b_qkv, dtype=np.float32)),
        "gatew": np.ascontiguousarray(np.asarray(L0_ffn_gate_w, dtype=np.float32)),
        "egu": np.ascontiguousarray(
            np.asarray(L0_ffn_experts_gate_up, dtype=np.float32)),
        "edown": np.ascontiguousarray(
            np.asarray(L0_ffn_experts_down, dtype=np.float32)),
        "wg": np.ascontiguousarray(np.asarray(L0_ffn_shared_gate_w, dtype=np.float32)),
        "wu": np.ascontiguousarray(np.asarray(L0_ffn_shared_up_w, dtype=np.float32)),
        "wd": np.ascontiguousarray(np.asarray(L0_ffn_shared_down_w, dtype=np.float32)),
        "segw": np.ascontiguousarray(
            np.asarray(L0_ffn_shared_expert_gate_w, dtype=np.float32)),
    }
    if _NC_CACHE[0] is None:
        _NC_CACHE[0] = build()
    nc = _NC_CACHE[0]
    in_maps = []
    for c in range(NCORES):
        m = {"x": np.ascontiguousarray(x[B * c:B * c + B].reshape(T, D))}
        m.update(weights)
        in_maps.append(m)
    res = run_bass_kernel_spmd(nc, in_maps, core_ids=list(range(NCORES)))
    shards = [np.asarray(r["out"], dtype=np.float32).reshape(B, S, D)
              for r in res.results]
    return np.ascontiguousarray(np.concatenate(shards, axis=0))
